# version 21
# baseline (speedup 1.0000x reference)
import numpy as np
import concourse.bass as bass
import concourse.mybir as mybir
from concourse.bass_utils import run_bass_kernel_spmd

F32 = mybir.dt.float32
F16 = mybir.dt.float16
F8 = mybir.dt.float8e4
I32 = mybir.dt.int32
OP = mybir.AluOpType
AF = mybir.ActivationFunctionType
AX = mybir.AxisListType.X

D = 2048
FF = 4096
TOK = 2048
NT = 16
EPS = 1e-5

ENGS = ("sync", "scalar", "vector", "tensor", "gpsimd")
DEBUG = False


class Prog:
    def __init__(self):
        self.ops = {e: [] for e in ENGS}
        self.cnt = {}

    def ch(self, name):
        self.cnt[name] = 0

    ENG_CH = {"vector": "V", "scalar": "C", "gpsimd": "G"}

    def emit(self, eng, fn, waits=(), inc=None, amt=1):
        if inc is None and eng in self.ENG_CH:
            inc = self.ENG_CH[eng]
            amt = 1
        tok = None
        if inc is not None:
            self.cnt[inc] += amt
            tok = (inc, self.cnt[inc])
        ws = tuple(w for w in waits if w is not None)
        self.ops[eng].append((fn, ws, inc, amt))
        return tok

    def barrier(self):
        return [(c, n) for c, n in self.cnt.items()]

    def emitters(self, sems):
        last = {}
        COMPUTE = ("V", "C", "G")

        def mk(name):
            def run(eng):
                pending = None
                for fn, waits, inc, amt in self.ops[name]:
                    if pending is not None:
                        s, v = pending
                        k = (name, s)
                        if last.get(k, 0) < v:
                            eng.wait_ge(sems[s], v)
                            last[k] = v
                        pending = None
                    for (s, v) in waits:
                        k = (name, s)
                        if last.get(k, 0) < v:
                            eng.wait_ge(sems[s], v)
                            last[k] = v
                    inst = fn(eng)
                    if inc is not None:
                        inst.then_inc(sems[inc], amt)
                        if inc in COMPUTE:
                            self_tok = None
                            pending = (inc, self._emitted_cnt[name][inc])
                for _ in ():
                    pass
            return run
        self._emitted_cnt = {e: {} for e in ENGS}
        for e in ENGS:
            cnt = {}
            vals = []
            for fn, waits, inc, amt in self.ops[e]:
                if inc is not None:
                    cnt[inc] = cnt.get(inc, 0) + amt
                    vals.append(cnt[inc])
                else:
                    vals.append(None)
            self._emitted_cnt[e] = (cnt, vals)

        def mk2(name):
            _, vals = self._emitted_cnt[name]

            def run(eng):
                pending = None
                for idx, (fn, waits, inc, amt) in enumerate(self.ops[name]):
                    if pending is not None:
                        s, v = pending
                        k = (name, s)
                        if last.get(k, 0) < v:
                            eng.wait_ge(sems[s], v)
                            last[k] = v
                        pending = None
                    for (s, v) in waits:
                        k = (name, s)
                        if last.get(k, 0) < v:
                            eng.wait_ge(sems[s], v)
                            last[k] = v
                    inst = fn(eng)
                    if inc is not None:
                        inst.then_inc(sems[inc], amt)
                        if inc in COMPUTE:
                            pending = (inc, vals[idx])
            return run
        return {e: mk2(e) for e in ENGS}


class Ring:
    def __init__(self, slots, semnames=None):
        self.slots = list(slots)
        self.n = len(self.slots)
        self.users = [[] for _ in self.slots]
        self.sems = semnames or [None] * self.n

    def acquire(self, i):
        s = i % self.n
        waits = self.users[s]
        self.users[s] = []
        return self.slots[s], waits

    def sem(self, i):
        return self.sems[i % self.n]

    def used(self, i, tok):
        if tok is not None:
            self.users[i % self.n].append(tok)


def build_nc():
    nc = bass.Bass("TRN2", target_bir_lowering=False, detect_race_conditions=False)

    x_own = nc.dram_tensor("x_own", [TOK, D], F32, kind="ExternalInput").ap()
    x_sib = nc.dram_tensor("x_sib", [TOK, D], F32, kind="ExternalInput").ap()
    Wv = nc.dram_tensor("Wv", [D, D], F32, kind="ExternalInput").ap()
    Wo = nc.dram_tensor("Wo", [D, D], F32, kind="ExternalInput").ap()
    wuT = nc.dram_tensor("wuT", [D, FF], F32, kind="ExternalInput").ap()
    wdT = nc.dram_tensor("wdT", [FF, D], F32, kind="ExternalInput").ap()
    bvT_h = nc.dram_tensor("bvT", [128, 16], F32, kind="ExternalInput").ap()
    boT_h = nc.dram_tensor("boT", [128, 16], F32, kind="ExternalInput").ap()
    id16_h = nc.dram_tensor("id16", [128, 128], F16, kind="ExternalInput").ap()
    id32_h = nc.dram_tensor("id32", [128, 128], F32, kind="ExternalInput").ap()
    ones1_h = nc.dram_tensor("ones1", [1, 128], F32, kind="ExternalInput").ap()
    onec_h = nc.dram_tensor("onec", [128, 1], F32, kind="ExternalInput").ap()
    y_out = nc.dram_tensor("y", [TOK, D], F32, kind="ExternalOutput").ap()
    k2_hbm = nc.dram_tensor("k2s", [TOK, FF], F16, kind="Internal").ap()
    td_hbm = nc.dram_tensor("tds", [FF, D], F8, kind="Internal").ap()
    us_hbm = nc.dram_tensor("us", [16, 128], F32, kind="Internal").ap()
    if DEBUG:
        dbg_crep = nc.dram_tensor("dbg_crep", [128, D], F32, kind="ExternalOutput").ap()
        dbg_srep = nc.dram_tensor("dbg_srep", [128, 8], F32, kind="ExternalOutput").ap()
        dbg_kf = nc.dram_tensor("dbg_kf", [128, D], F16, kind="ExternalOutput").ap()
        dbg_hb = nc.dram_tensor("dbg_hb", [128, FF], F32, kind="ExternalOutput").ap()
        dbg_am2 = nc.dram_tensor("dbg_am2", [128, 16], F32, kind="ExternalOutput").ap()
        dbg_wt = nc.dram_tensor("dbg_wt", [128, 4096], F8, kind="ExternalOutput").ap()
        dbg_x1 = nc.dram_tensor("dbg_x1", [128, D], F32, kind="ExternalOutput").ap()
        dbg_tv = nc.dram_tensor("dbg_tv", [128, 64], F32, kind="ExternalOutput").ap()
        dbg_xrow = nc.dram_tensor("dbg_xrow", [1, D], F32, kind="ExternalOutput").ap()
        dbg_rep = nc.dram_tensor("dbg_rep", [128, D], F32, kind="ExternalOutput").ap()
        dbg_ucol = nc.dram_tensor("dbg_ucol", [128, 16], F32, kind="ExternalOutput").ap()
        dbg_uT = nc.dram_tensor("dbg_uT", [16, 128], F32, kind="ExternalOutput").ap()
        dbg_m1 = nc.dram_tensor("dbg_m1", [128, D], F16, kind="ExternalOutput").ap()
    cs_hbm = nc.dram_tensor("cs", [16, 128], F32, kind="Internal").ap()

    p = Prog()
    DMACH = ("LC", "L0", "L1", "W0", "W1", "LB", "LT", "K0", "K1",
             "U0", "U1", "U2", "U3", "L2", "L3", "D0", "D1", "SB", "SY", "SK", "SD0", "SD1", "SG")
    for c in ("T", "V", "C", "G") + DMACH:
        p.ch(c)

    from contextlib import ExitStack
    es = ExitStack()
    with es:
        id16 = es.enter_context(nc.sbuf_tensor("sb_id16", [128, 128], F16))
        id32 = es.enter_context(nc.sbuf_tensor("sb_id32", [128, 128], F32))
        ones1 = es.enter_context(nc.sbuf_tensor("sb_ones1", [1, 128], F32))
        onec = es.enter_context(nc.sbuf_tensor("sb_onec", [128, 1], F32))
        bvT = es.enter_context(nc.sbuf_tensor("sb_bvT", [128, 16], F32))
        boT = es.enter_context(nc.sbuf_tensor("sb_boT", [128, 16], F32))
        c_rep = es.enter_context(nc.sbuf_tensor("c_rep", [128, D], F32))
        am2c = es.enter_context(nc.sbuf_tensor("am2c", [128, 16], F32))
        wabs_u = es.enter_context(nc.sbuf_tensor("wabs_u", [128, 32], F32))
        wabs_d = es.enter_context(nc.sbuf_tensor("wabs_d", [128, 32], F32))
        ucol = es.enter_context(nc.sbuf_tensor("ucol", [128, 16], F32))
        ccol = es.enter_context(nc.sbuf_tensor("ccol", [128, 16], F32))
        uTsb = es.enter_context(nc.sbuf_tensor("uT", [16, 128], F32))
        cTsb = es.enter_context(nc.sbuf_tensor("cT", [16, 128], F32))
        tiny = es.enter_context(nc.sbuf_tensor("tiny", [128, 64], F32))
        tinyC = es.enter_context(nc.sbuf_tensor("tinyC", [128, 16], F32))
        xsall = es.enter_context(nc.sbuf_tensor("xsall", [128, 32], F32))
        ssqall = es.enter_context(nc.sbuf_tensor("ssqall", [128, 32], F32))
        aall = es.enter_context(nc.sbuf_tensor("aall", [128, 32], F32))
        maall = es.enter_context(nc.sbuf_tensor("maall", [128, 32], F32))
        p1s = es.enter_context(nc.sbuf_tensor("p1s", [128, 128], F32))
        srep = es.enter_context(nc.sbuf_tensor("srep", [128, 8], F32))
        dall = es.enter_context(nc.sbuf_tensor("dall", [128, 16], F32))
        t8a = es.enter_context(nc.sbuf_tensor("t8a", [128, 2048], F8))
        mkq = es.enter_context(nc.sbuf_tensor("mkq8", [128, 1024], F8))
        mk8 = es.enter_context(nc.sbuf_tensor("mk88", [128, 1024], F8))
        t8b = es.enter_context(nc.sbuf_tensor("t8b", [128, 2048], F8))
        pf32 = es.enter_context(nc.sbuf_tensor("pf32", [128, 23680], F32))
        pf16 = es.enter_context(nc.sbuf_tensor("pf16", [128, 12288], F16))
        pi32 = es.enter_context(nc.sbuf_tensor("pi32", [128, 2048], I32))
        wT = es.enter_context(nc.sbuf_tensor("wT", [128, 65536], F8))
        PA = es.enter_context(nc.psum_tensor("PA", [128, 2048], F32))
        PM = es.enter_context(nc.psum_tensor("PM", [128, 512], F32))
        PT2 = es.enter_context(nc.psum_tensor("PT2", [128, 512], F32))
        PS = es.enter_context(nc.psum_tensor("PS", [128, 512], F32))
        sems = {}
        for c in ("T", "V", "C", "G") + DMACH:
            sems[c] = es.enter_context(nc.semaphore("sem" + c))
        block = es.enter_context(nc.Block())
        def f32(off, n):
            return pf32[:, off:off + n]

        TC = {}

        def tc(name):
            if name not in TC:
                TC[name] = len(TC)
                assert len(TC) <= 16
            i = TC[name]
            return tinyC[:, i:i + 1]

        TV = {}

        def tv(name):
            if name not in TV:
                TV[name] = len(TV)
                assert len(TV) <= 56
            i = TV[name]
            return tiny[:, i:i + 1]

        onesm = f32(23552, 128)
        om = p.emit("vector", lambda e: e.memset(onesm, 1.0), inc="V")
        for dst, src in ((id16, id16_h), (id32, id32_h), (ones1, ones1_h),
                         (onec, onec_h), (bvT, bvT_h), (boT, boT_h)):
            p.emit("sync", lambda e, d=dst, s=src: e.dma_start(d[:], s[:]), inc="LC", amt=16)
        tok_const = ("LC", p.cnt["LC"])

        xb_ring = Ring([f32(0, D), f32(2048, D), f32(4096, D)], ["L0", "L1", "L2"])
        wu_ring = Ring([f32(6144, 4096), f32(10240, 4096)], ["U0", "U1"])
        wdnB_ring = Ring([f32(14336, 4096), f32(18432, 4096)], ["D0", "D1"])
        cdump = f32(12288, D)
        vdump = f32(14336, D)

        last_mm = None
        last_bs = None

        wdT2 = wdT.rearrange("(a b) c -> a (b c)", b=2)
        td2 = td_hbm.rearrange("(a b) c -> a (b c)", b=2)
        bnb = p1s[:, 64:88]
        wu_i = [0]
        wd_i = [0]
        absu_last = [None]
        absd_last2 = [None]

        def emit_absu(r):
            i = wu_i[0]; wu_i[0] += 1
            slot, ws = wu_ring.acquire(i)
            ld = p.emit("scalar", lambda e, d=slot, r=r: e.dma_start(
                d[:], wuT[128 * r:128 * (r + 1), :]), waits=ws, inc=wu_ring.sem(i), amt=16)
            p.emit("scalar", lambda e, s=slot, r=r: e.activation(
                s[:, 0:2048], s[:, 0:2048], AF.Abs, accum_out=wabs_u[:, 2 * r:2 * r + 1]),
                waits=[ld], inc="C")
            a2 = p.emit("scalar", lambda e, s=slot, r=r: e.activation(
                s[:, 2048:4096], s[:, 2048:4096], AF.Abs, accum_out=wabs_u[:, 2 * r + 1:2 * r + 2]),
                inc="C")
            wu_ring.used(i, a2)
            absu_last[0] = a2

        def emit_absd(r):
            i = wd_i[0]; wd_i[0] += 1
            slot, ws = wdnB_ring.acquire(i)
            ld = p.emit("gpsimd", lambda e, d=slot, r=r: e.dma_start(
                d[:], wdT2[128 * r:128 * (r + 1), :]), waits=ws, inc=wdnB_ring.sem(i), amt=16)
            p.emit("scalar", lambda e, s=slot, r=r: e.activation(
                s[:, 0:2048], s[:, 0:2048], AF.Abs, accum_out=wabs_d[:, 2 * r:2 * r + 1]),
                waits=[ld], inc="C")
            a2 = p.emit("scalar", lambda e, s=slot, r=r: e.activation(
                s[:, 2048:4096], s[:, 2048:4096], AF.Abs, accum_out=wabs_d[:, 2 * r + 1:2 * r + 2]),
                inc="C")
            wdnB_ring.used(i, a2)
            absd_last2[0] = a2

        ma_last = None
        for t in range(32):
            src_t = x_own if t < 16 else x_sib
            r0 = (t % 16) * 128
            slot, ws = xb_ring.acquire(t)
            ld = p.emit("sync", lambda e, d=slot, s=src_t, r=r0: e.dma_start(d[:], s[r:r + 128, :]),
                        waits=ws + [tok_const], inc=xb_ring.sem(t), amt=16)
            for q in range(4):
                p.emit("vector", lambda e, xbv=slot, q=q: e.bn_stats(
                    bnb[:, 6 * q:6 * (q + 1)], xbv[:, 512 * q:512 * (q + 1)]),
                    waits=[ld], inc="V")
            mvt = p1s[:, 2 * t:2 * t + 2]
            ag = p.emit("vector", lambda e, mvt=mvt: e.bn_aggr(mvt, bnb), inc="V")
            p.emit("vector", lambda e, mvt=mvt: e.tensor_scalar(
                tv("ve"), mvt[:, 1:2], 1.0, EPS, OP.mult, OP.add))
            rv = p.emit("vector", lambda e: e.reciprocal(tv("rq"), tv("ve")), inc="V")
            rc = p.emit("scalar", lambda e, t=t: e.sqrt(aall[:, t:t + 1], tv("rq")),
                        waits=[rv], inc="C")
            ma_last = p.emit("vector", lambda e, mvt=mvt, t=t: e.tensor_tensor(
                maall[:, t:t + 1], mvt[:, 0:1], aall[:, t:t + 1], OP.mult),
                waits=[rc], inc="V")
            for q in range(4):
                last_mm = p.emit("tensor", lambda e, xbv=slot, q=q, t=t,
                                 s0=(t == 0), s1=(t == 31): e.matmul(
                    PA[0:1, 512 * q:512 * (q + 1)], aall[:, t:t + 1],
                    xbv[:, 512 * q:512 * (q + 1)], start=s0, stop=s1),
                    waits=[rc, ld], inc="T")
                xb_ring.used(t, last_mm)
            xb_ring.used(t, ag)
            if t % 2 == 0:
                emit_absu(t // 2)
            else:
                emit_absd(t // 2)

        svd = p.emit("vector", lambda e: e.tensor_reduce(tv("wsd"), wabs_d[:], AX, OP.add),
                     waits=[absd_last2[0]], inc="V")
        tkd = p.emit("tensor", lambda e: e.matmul(PS[:, 17:18], onesm, tv("wsd"),
                                                  start=True, stop=True), waits=[svd, om], inc="T")
        rrd = p.emit("vector", lambda e: e.tensor_copy(tv("draw"), PS[:, 17:18]),
                     waits=[tkd], inc="V")
        p.emit("vector", lambda e: e.tensor_scalar(srep[:, 2:3], tv("draw"),
                                                   1.0 / (D * FF), EPS, OP.mult, OP.max))
        sdn_tok = p.emit("vector", lambda e: e.reciprocal(srep[:, 3:4], srep[:, 2:3]), inc="V")
        thd_tok = p.emit("vector", lambda e: e.tensor_scalar(
            srep[:, 5:6], srep[:, 2:3], 0.5, None, OP.mult), inc="V")
        sdn_rep2 = srep[:, 2:3]
        ivd_rep2 = srep[:, 3:4]
        thd_rep = srep[:, 5:6]

        sv = p.emit("vector", lambda e: e.tensor_reduce(tv("wsu"), wabs_u[:], AX, OP.add),
                    waits=[absu_last[0]], inc="V")
        tksu = p.emit("tensor", lambda e: e.matmul(PS[:, 32:33], onesm, tv("wsu"),
                                                   start=True, stop=True),
                      waits=[sv, om], inc="T")
        rr = p.emit("vector", lambda e: e.tensor_copy(tv("wraw"), PS[:, 32:33]),
                    waits=[tksu], inc="V")
        p.emit("vector", lambda e: e.tensor_scalar(srep[:, 0:1], tv("wraw"),
                                                   1.0 / (D * FF), EPS, OP.mult, OP.max))
        bu = p.emit("vector", lambda e: e.reciprocal(srep[:, 1:2], srep[:, 0:1]), inc="V")
        thu_tok = p.emit("vector", lambda e: e.tensor_scalar(
            srep[:, 4:5], srep[:, 0:1], 0.5, None, OP.mult), inc="V")
        sup_rep = srep[:, 0:1]
        ivu_rep = srep[:, 1:2]
        thu_rep = srep[:, 4:5]

        last_bs = p.emit("tensor", lambda e: e.matmul(PS[:, 0:32], onesm, maall[:],
                                                      start=True, stop=True),
                         waits=[rr, om], inc="T")

        m1b = pf16[:, 0:4096]
        m2b = pf16[:, 4096:8192]
        t8_rP = Ring([t8a, t8b], ["SD0", "SD1"])
        ternV_last = None
        ternd_P_toks = []
        for r in range(16):
            i = wu_i[0]; wu_i[0] += 1
            slot, ws = wu_ring.acquire(i)
            ld = p.emit("scalar", lambda e, d=slot, r=r: e.dma_start(
                d[:], wuT[128 * r:128 * (r + 1), :]), waits=ws + [thu_tok], inc=wu_ring.sem(i), amt=16)
            p.emit("vector", lambda e, s=slot: e.tensor_scalar(
                m1b, s, -1.0, thu_rep, OP.mult, OP.is_gt), waits=[ld, thu_tok])
            ternV_last = p.emit("vector", lambda e, s=slot, r=r: e.scalar_tensor_tensor(
                wT[:, 4096 * r:4096 * (r + 1)], s, thu_rep, m1b, OP.is_gt, OP.subtract), inc="V")
            wu_ring.used(i, ternV_last)

            i = wd_i[0]; wd_i[0] += 1
            slot, ws = wdnB_ring.acquire(i)
            ld = p.emit("gpsimd", lambda e, d=slot, r=r: e.dma_start(
                d[:], wdT2[128 * r:128 * (r + 1), :]), waits=ws + [thd_tok],
                inc=wdnB_ring.sem(i), amt=16)
            p.emit("vector", lambda e, s=slot: e.tensor_scalar(
                m1b, s, -1.0, thd_rep, OP.mult, OP.is_gt), waits=[ld, thd_tok])
            ta, taw = t8_rP.acquire(2 * r)
            t1 = p.emit("vector", lambda e, s=slot, o=ta: e.scalar_tensor_tensor(
                o[:], s[:, 0:2048], thd_rep, m1b[:, 0:2048], OP.is_gt, OP.subtract),
                waits=taw, inc="V")
            sa = p.emit("sync", lambda e, r=r, o=ta: e.dma_start(
                td2[128 * r:128 * (r + 1), 0:2048], o[:]), waits=[t1],
                inc=t8_rP.sem(2 * r), amt=16)
            t8_rP.used(2 * r, sa)
            tb, tbw = t8_rP.acquire(2 * r + 1)
            t2 = p.emit("vector", lambda e, s=slot, o=tb: e.scalar_tensor_tensor(
                o[:], s[:, 2048:4096], thd_rep, m1b[:, 2048:4096], OP.is_gt, OP.subtract),
                waits=tbw, inc="V")
            sb = p.emit("sync", lambda e, r=r, o=tb: e.dma_start(
                td2[128 * r:128 * (r + 1), 2048:4096], o[:]), waits=[t2],
                inc=t8_rP.sem(2 * r + 1), amt=16)
            t8_rP.used(2 * r + 1, sb)
            wdnB_ring.used(i, t2)
            ternV_last = t2
            ternd_P_toks.append(sb)

        xrow = pf32[0:1, 18432:18432 + D]
        e1 = p.emit("vector", lambda e: e.tensor_copy(xrow, PA[0:1, :]), waits=[last_mm], inc="V")
        bs = p.emit("vector", lambda e: e.tensor_reduce(tv("bsr"), PS[:, 0:32], AX, OP.add),
                    waits=[last_bs], inc="V")
        rep = f32(16384, D)
        brd = []
        for q in range(4):
            brd.append(p.emit("tensor", lambda e, q=q: e.matmul(
                PA[:, 512 * q:512 * (q + 1)], ones1[:], xrow[0:1, 512 * q:512 * (q + 1)],
                start=True, stop=True), waits=[e1], inc="T"))
        re1 = p.emit("vector", lambda e: e.tensor_scalar(rep, PA[:], tv("bsr"), 1.0 / 4096.0,
                                                         OP.subtract, OP.mult),
                     waits=brd + [bs], inc="V")
        if DEBUG:
            p.emit("scalar", lambda e: e.dma_start(dbg_xrow[:], rep[0:1, :]), waits=[re1], inc="SG", amt=16)

        if DEBUG:
            p.emit("scalar", lambda e: e.dma_start(dbg_rep[:], rep), waits=[re1], inc="SG", amt=16)
        wvo_ring = Ring([f32(8192, D), f32(10240, D)], ["W0", "W1"])
        ttr_last = None
        for t in range(16):
            slot, ws = wvo_ring.acquire(t)
            ld = p.emit("sync", lambda e, d=slot, t=t: e.dma_start(d[:], Wv[128 * t:128 * (t + 1), :]),
                        waits=ws, inc=wvo_ring.sem(t), amt=16)
            ttr_last = p.emit("vector", lambda e, s=slot, t=t: e.scalar_tensor_tensor(
                vdump, s, 1.0, rep, OP.mult, OP.mult, accum_out=ucol[:, t:t + 1]),
                waits=[ld, re1], inc="V")
            wvo_ring.used(t, ttr_last)
        ub = p.emit("vector", lambda e: e.tensor_tensor(ucol[:], ucol[:], bvT[:], OP.add),
                    waits=[ttr_last], inc="V")
        if DEBUG:
            p.emit("scalar", lambda e: e.dma_start(dbg_ucol[:], ucol[:]), waits=[ub], inc="SG", amt=16)
        tk = p.emit("tensor", lambda e: e.matmul(PM[0:16, 0:128], ucol[:], id32[:],
                                                 start=True, stop=True), waits=[ub], inc="T")
        ue = p.emit("vector", lambda e: e.tensor_copy(uTsb[:], PM[0:16, 0:128]), waits=[tk], inc="V")
        if DEBUG:
            p.emit("scalar", lambda e: e.dma_start(dbg_uT[:], uTsb[:]), waits=[ue], inc="SG", amt=16)
        ust = p.emit("scalar", lambda e: e.dma_start(us_hbm[:], uTsb[:]), waits=[ue], inc="SB", amt=16)
        uld = p.emit("sync", lambda e: e.dma_start(
            xrow, us_hbm.rearrange("a b -> (a b)")[None, :]), waits=[ust, e1] + brd, inc="LB", amt=16)
        brd2 = []
        for q in range(4):
            brd2.append(p.emit("tensor", lambda e, q=q: e.matmul(
                PA[:, 512 * q:512 * (q + 1)], ones1[:], xrow[0:1, 512 * q:512 * (q + 1)],
                start=True, stop=True), waits=[uld, re1], inc="T"))
        re2 = p.emit("scalar", lambda e: e.activation(rep, PA[:], AF.Copy),
                     waits=brd2 + [ttr_last], inc="C")

        tto_last = None
        for t in range(16):
            slot, ws = wvo_ring.acquire(16 + t)
            ld = p.emit("sync", lambda e, d=slot, t=t: e.dma_start(d[:], Wo[128 * t:128 * (t + 1), :]),
                        waits=ws, inc=wvo_ring.sem(16 + t), amt=16)
            tto_last = p.emit("vector", lambda e, s=slot, t=t: e.scalar_tensor_tensor(
                vdump, s, 1.0, rep, OP.mult, OP.mult, accum_out=ccol[:, t:t + 1]),
                waits=[ld, re2], inc="V")
            wvo_ring.used(16 + t, tto_last)
        cb = p.emit("vector", lambda e: e.tensor_tensor(ccol[:], ccol[:], boT[:], OP.add),
                    waits=[tto_last], inc="V")
        tk = p.emit("tensor", lambda e: e.matmul(PM[0:16, 0:128], ccol[:], id32[:],
                                                 start=True, stop=True), waits=[cb, ue], inc="T")
        ce = p.emit("vector", lambda e: e.tensor_copy(cTsb[:], PM[0:16, 0:128]), waits=[tk], inc="V")
        cst = p.emit("scalar", lambda e: e.dma_start(cs_hbm[:], cTsb[:]), waits=[ce], inc="SB", amt=16)
        cld = p.emit("sync", lambda e: e.dma_start(
            xrow, cs_hbm.rearrange("a b -> (a b)")[None, :]), waits=[cst, uld] + brd2, inc="LB", amt=16)
        brd3 = []
        for q in range(4):
            brd3.append(p.emit("tensor", lambda e, q=q: e.matmul(
                PA[:, 512 * q:512 * (q + 1)], ones1[:], xrow[0:1, 512 * q:512 * (q + 1)],
                start=True, stop=True), waits=[cld, re2], inc="T"))
        crep_tok = p.emit("scalar", lambda e: e.activation(c_rep[:], PA[:], AF.Copy),
                          waits=brd3, inc="C")

        bar = p.barrier()

        xbu_ring = Ring([f32(0, D), f32(2048, D)], ["L0", "L1"])
        x1b = f32(4096, D)
        hb = f32(8192, FF)
        zq = f32(18432, 1024)
        mq = f32(19456, 1024)
        e8 = f32(20480, 1024)
        z8 = f32(21504, 1024)
        s8 = f32(22528, 1024)
        wdn_ring = Ring([f32(14336, D), f32(16384, D)], ["D0", "D1"])
        kf = pf16[:, 0:2048]
        dm1 = pf16[:, 2048:4096]
        dm2 = pf16[:, 4096:6144]
        kT_one = pf16[:, 6144:8192]
        k2b = pf16[:, 8192:12288]
        ziq = pi32[:, 0:1024]
        zi8 = pi32[:, 1024:2048]

        psU = [PA[:, 0:1024], PA[:, 1024:2048]]
        psU_users = [[crep_tok], [crep_tok]]
        psT = [PM, PT2]
        psT_users = [[ce, bu], [ce, bu]]

        wdn_items = []
        wdn_iter = iter(wdn_items)
        wdn_i = [0]
        absd_last = [None]
        sdn_done = [None]
        ternd_toks = []
        t8_ring = Ring([t8a, t8b], ["SD0", "SD1"])
        sdn_rep = srep[:, 2:3]
        ivd_rep = srep[:, 3:4]
        dm_prev = [None]

        def emit_wdn(n):
            for _ in range(n):
                it = next(wdn_iter, None)
                if it is None:
                    return
                kind, r = it
                if kind == "sdn":
                    sv = p.emit("vector", lambda e: e.tensor_reduce(tv("wsd"), wabs_d[:], AX, OP.add),
                                waits=[absd_last2[0]] + bar, inc="V")
                    tk = p.emit("tensor", lambda e: e.matmul(PS[:, 17:18], onesm, tv("wsd"),
                                                             start=True, stop=True),
                                waits=[sv] + bar, inc="T")
                    rr = p.emit("vector", lambda e: e.tensor_copy(tv("draw"), PS[:, 17:18]),
                                waits=[tk], inc="V")
                    p.emit("vector", lambda e: e.tensor_scalar(srep[:, 2:3], tv("draw"),
                                                               1.0 / (D * FF), EPS, OP.mult, OP.max))
                    sdn_done[0] = p.emit("vector", lambda e: e.reciprocal(srep[:, 3:4], srep[:, 2:3]),
                                         inc="V")
                else:
                    i = wdn_i[0]; wdn_i[0] += 1
                    slot, ws = wdn_ring.acquire(i)
                    ld = p.emit("sync", lambda e, d=slot, r=r: e.dma_start(
                        d[:], wdT[128 * r:128 * (r + 1), :]), waits=ws + [sdn_done[0]],
                        inc=wdn_ring.sem(i), amt=16)
                    g1 = p.emit("vector", lambda e, s=slot: e.tensor_scalar(
                        dm1, s, ivd_rep, 0.5, OP.mult, OP.is_gt),
                        waits=[ld, sdn_done[0], dm_prev[0]])
                    g2 = p.emit("vector", lambda e, s=slot: e.tensor_scalar(
                        dm2, s, ivd_rep, -0.5, OP.mult, OP.is_lt), waits=[ld])
                    t8, t8w = t8_ring.acquire(len(ternd_toks))
                    tt = p.emit("vector", lambda e, o=t8: e.tensor_tensor(o[:], dm1, dm2, OP.subtract),
                                waits=[g2, g1] + t8w, inc="V")
                    dm_prev[0] = tt
                    st = p.emit("scalar", lambda e, r=r, o=t8: e.dma_start(
                        td_hbm[128 * r:128 * (r + 1), :], o[:]), waits=[tt],
                        inc=t8_ring.sem(len(ternd_toks)), amt=16)
                    t8_ring.used(len(ternd_toks), st)
                    wdn_ring.used(i, tt)
                    ternd_toks.append(st)

        prev_sq = [None]
        kf_free = [None]
        k2b_free = [None]
        st_state = {}

        def emit_front(i):
            slot, ws = xbu_ring.acquire(i)
            ld = p.emit("sync", lambda e, d=slot, i=i: e.dma_start(d[:], x_own[128 * i:128 * (i + 1), :]),
                        waits=ws + bar, inc=xbu_ring.sem(i), amt=16)
            x1 = p.emit("vector", lambda e, s=slot: e.scalar_tensor_tensor(
                x1b, s, 0.0, c_rep[:], OP.add, OP.add, accum_out=tv("xs")),
                waits=[ld, crep_tok, prev_sq[0]] + bar, inc="V")
            xbu_ring.used(i, x1)
            p.emit("vector", lambda e: e.tensor_reduce(tv("rmax"), x1b, AX, OP.max))
            p.emit("vector", lambda e: e.tensor_reduce(tv("rmin"), x1b, AX, OP.min))
            prev_sq[0] = p.emit("scalar", lambda e: e.activation(
                cdump, x1b, AF.Square, accum_out=tc("ssq")), waits=[x1] + bar, inc="C")
            p.emit("vector", lambda e: e.tensor_scalar(tv("mean"), tv("xs"), 1.0 / D, None, OP.mult))
            p.emit("vector", lambda e: e.tensor_tensor(tv("me2"), tv("mean"), tv("mean"), OP.mult))
            p.emit("vector", lambda e: e.tensor_scalar(tv("v1"), tc("ssq"), 1.0 / D, None, OP.mult),
                   waits=[prev_sq[0]])
            p.emit("vector", lambda e: e.tensor_tensor(tv("var"), tv("v1"), tv("me2"), OP.subtract))
            p.emit("vector", lambda e: e.tensor_scalar(tv("ve"), tv("var"), EPS, None, OP.add))
            rv = p.emit("vector", lambda e: e.reciprocal(tv("rq"), tv("ve")), inc="V")
            rs = p.emit("scalar", lambda e: e.sqrt(tc("rstd"), tv("rq")), waits=[rv], inc="C")
            p.emit("vector", lambda e: e.tensor_tensor(tv("t1"), tv("rmax"), tv("mean"), OP.subtract),
                   waits=[rs])
            p.emit("vector", lambda e: e.tensor_tensor(tv("t2"), tv("mean"), tv("rmin"), OP.subtract))
            p.emit("vector", lambda e: e.tensor_tensor(tv("t3"), tv("t1"), tv("t2"), OP.max))
            p.emit("vector", lambda e: e.tensor_tensor(tv("amax"), tv("t3"), tc("rstd"), OP.mult))
            amc = tv("amc%d" % (i % 2))
            p.emit("vector", lambda e, amc=amc: e.tensor_scalar(amc, tv("amax"), EPS, None, OP.max))
            p.emit("vector", lambda e, amc=amc: e.tensor_scalar(tv("r127"), amc, 1.0 / 127.0, None, OP.mult))
            p.emit("vector", lambda e: e.reciprocal(tv("scl"), tv("r127")))
            p.emit("vector", lambda e: e.tensor_tensor(tv("sa"), tc("rstd"), tv("scl"), OP.mult))
            p.emit("vector", lambda e: e.tensor_tensor(tv("nb0"), tv("mean"), tv("sa"), OP.mult))
            nb = p.emit("vector", lambda e: e.tensor_scalar(tv("nbias"), tv("nb0"), -1.0, None, OP.mult),
                        inc="V")
            kq_last = None
            zcs = {}

            def q_zv(q):
                s_ = q % 2
                bsl = slice(512 * s_, 512 * (s_ + 1))
                zh = zq[:, bsl]
                sl = slice(512 * q, 512 * (q + 1))
                zv = p.emit("vector", lambda e, sl=sl, zh=zh: e.tensor_scalar(
                    zh, x1b[:, sl], tv("sa"), tv("nbias"), OP.mult, OP.add),
                    waits=[nb] + ([kf_free[0]] if q <= 1 else []), inc="V")
                zcs[q] = p.emit("scalar", lambda e, zh=zh, zih=ziq[:, bsl]: e.activation(
                    zih, zh, AF.Copy), waits=[zv] + bar, inc="C")

            def q_fix(q):
                s_ = q % 2
                bsl = slice(512 * s_, 512 * (s_ + 1))
                zh = zq[:, bsl]; mh = mq[:, bsl]; zih = ziq[:, bsl]
                sl = slice(512 * q, 512 * (q + 1))
                p.emit("vector", lambda e, zh=zh, zih=zih, mh2=mh: e.tensor_tensor(
                    mh2, zh, zih, OP.subtract), waits=[zcs[q]])
                p.emit("vector", lambda e, mh2=mh, zih=zih, zh=zh: e.scalar_tensor_tensor(
                    zh, mh2, -0.5, zih, OP.is_lt, OP.subtract))
                return p.emit("vector", lambda e, sl=sl, mh2=mh, zh=zh: e.scalar_tensor_tensor(
                    kf[:, sl], mh2, 0.5, zh, OP.is_gt, OP.subtract), inc="V")

            q_zv(0); q_zv(1)
            q_fix(0)
            q_zv(2)
            q_fix(1)
            q_zv(3)
            q_fix(2)
            kq_last = q_fix(3)
            kT, kws = kT_one, []
            ktr_evs = []
            tr_last = None
            for g4 in range(4):
                pt = psT[g4 % 2]
                tws = psT_users[g4 % 2]
                psT_users[g4 % 2] = []
                mts = []
                for c in range(4):
                    cc = g4 * 4 + c
                    tr_last = p.emit("tensor", lambda e, cc=cc, c=c, pt=pt: e.matmul(
                        pt[:, 128 * c:128 * (c + 1)], kf[:, 128 * cc:128 * (cc + 1)], id16[:],
                        start=True, stop=True),
                        waits=[kq_last] + tws + kws + bar, inc="T")
                    mts.append(tr_last)
                ev = p.emit("scalar", lambda e, g4=g4, pt=pt, kT=kT: e.activation(
                    kT[:, 512 * g4:512 * (g4 + 1)], pt[:], AF.Copy), waits=mts, inc="C")
                psT_users[g4 % 2].append(ev)
                ktr_evs.append(ev)
            kf_free[0] = tr_last
            st_state[i] = {"kT": kT, "ktr": ktr_evs, "amc": amc}

        def emit_gamma(i):
            st = st_state[i]
            sg_prev = st_state.get(i - 1, {}).get("sg")
            p.emit("vector", lambda e: e.tensor_scalar(tv("Mr"), tv("Mq0"), 0.0, None, OP.max),
                   waits=st["mr"] + ([sg_prev] if sg_prev else []))
            p.emit("vector", lambda e, amc=st["amc"]: e.tensor_scalar(
                tv("ba"), amc, sup_rep, 1.0 / 127.0, OP.mult, OP.mult))
            p.emit("vector", lambda e: e.tensor_tensor(tv("bm"), tv("ba"), tv("Mr"), OP.mult))
            p.emit("vector", lambda e: e.tensor_tensor(tv("am2"), tv("bm"), tv("bm"), OP.mult))
            p.emit("vector", lambda e, i=i: e.tensor_scalar(
                am2c[:, i:i + 1], tv("am2"), EPS, None, OP.max))
            p.emit("vector", lambda e: e.tensor_tensor(tv("rb2"), tv("ba"), tv("ba"), OP.mult))
            p.emit("vector", lambda e, i=i: e.reciprocal(tv("rc"), am2c[:, i:i + 1]))
            p.emit("vector", lambda e: e.tensor_tensor(tv("g0"), tv("rb2"), tv("rc"), OP.mult))
            gv = p.emit("vector", lambda e: e.tensor_scalar(tv("g127"), tv("g0"), 127.0, None, OP.mult),
                        inc="V")
            st["sg"] = p.emit("scalar", lambda e: e.sqrt(tc("sqg"), tv("g127")), waits=[gv], inc="C")
            st["pk20"] = st_state.get(i - 1, {}).get("pk20")
            st["pk21"] = st_state.get(i - 1, {}).get("pk21")

        def emit_epi_chunk(i, o):
            st = st_state[i]
            s_ = o % 2
            bsl = slice(512 * s_, 512 * (s_ + 1))
            eh = e8[:, bsl]; zh = z8[:, bsl]; zih = zi8[:, bsl]
            sh = s8[:, bsl]; mh = mk8[:, 512 * s_:512 * (s_ + 1)]
            sl = slice(512 * o, 512 * (o + 1))
            pk = st["pk2%d" % s_]
            ee = p.emit("scalar", lambda e, sl=sl, eh=eh: e.activation(
                eh, hb[:, sl], AF.Relu, scale=tc("sqg")), waits=[st["sg"]], inc="C")
            zz = p.emit("scalar", lambda e, eh=eh, zh=zh: e.square(zh, eh), waits=[ee, pk], inc="C")
            zc = p.emit("scalar", lambda e, zh=zh, zih=zih: e.activation(zih, zh, AF.Copy),
                        waits=[zz], inc="C")
            p.emit("vector", lambda e, zh=zh, zih=zih, sh=sh: e.tensor_tensor(
                sh, zh, zih, OP.subtract),
                waits=[zc] + ([k2b_free[0]] if o <= 1 else []))
            p.emit("vector", lambda e, sh=sh, zih=zih, zh=zh: e.scalar_tensor_tensor(
                zh, sh, -0.5, zih, OP.is_lt, OP.subtract))
            tok = p.emit("vector", lambda e, sl=sl, sh=sh, zh=zh: e.scalar_tensor_tensor(
                k2b[:, sl], sh, 0.5, zh, OP.is_gt, OP.subtract), waits=bar, inc="V")
            st["pk2%d" % s_] = tok
            st.setdefault("k2toks", {})[o // 2 if False else o] = tok
            return tok

        def emit_mm_quarter(i, q):
            st = st_state[i]
            pu = psU[q % 2]
            uws = psU_users[q % 2]
            psU_users[q % 2] = []
            mm_last = None
            for c in range(16):
                for j in range(2):
                    mm_last = p.emit("tensor", lambda e, c=c, j=j, q=q, pu=pu, kT=st["kT"]: e.matmul(
                        pu[:, 512 * j:512 * (j + 1)],
                        kT[:, 128 * c:128 * (c + 1)],
                        wT[:, 4096 * c + 1024 * q + 512 * j:4096 * c + 1024 * q + 512 * (j + 1)],
                        start=(c == 0), stop=(c == 15)),
                        waits=(st["ktr"] + uws + [ternV_last] + bar) if c == 0 else (), inc="T")
            st.setdefault("mmlast", []).append(mm_last)

        def emit_evac(i, q):
            st = st_state[i]
            pu = psU[q % 2]
            hb_guard = (st_state.get(i - 1, {}).get("mr") or [None] * 4)[q]
            if q < 2:
                ev = p.emit("vector", lambda e, q=q, pu=pu: e.tensor_copy(
                    hb[:, 1024 * q:1024 * (q + 1)], pu[:]),
                    waits=[st["mmlast"][q], hb_guard], inc="V")
            else:
                ev = p.emit("scalar", lambda e, q=q, pu=pu: e.activation(
                    hb[:, 1024 * q:1024 * (q + 1)], pu[:], AF.Copy),
                    waits=[st["mmlast"][q], hb_guard], inc="C")
            psU_users[q % 2].append(ev)
            st.setdefault("evs", []).append(ev)

        for i in range(NT + 1):
            if i < NT:
                emit_front(i)
            if i >= 1:
                emit_gamma(i - 1)
            for q in range(4):
                if i >= 1:
                    emit_epi_chunk(i - 1, 2 * q)
                    emit_epi_chunk(i - 1, 2 * q + 1)
                if i < NT:
                    if q >= 2:
                        emit_evac(i, q - 2)
                    emit_mm_quarter(i, q)
            if i < NT:
                emit_evac(i, 2)
                emit_evac(i, 3)
            if i < NT:
                st = st_state[i]
                mr = p.emit("vector", lambda e: e.tensor_reduce(
                    tv("Mq0"), hb[:], AX, OP.max), waits=st["evs"], inc="V")
                st["mr"] = [mr, mr, mr, mr]
            if i >= 1:
                k2b_free[0] = p.emit("scalar", lambda e, j=i - 1: e.dma_start(
                    k2_hbm[128 * j:128 * (j + 1), :], k2b),
                    waits=[st_state[i - 1]["k2toks"][6], st_state[i - 1]["k2toks"][7]],
                    inc="SK", amt=16)
                st_state.pop(i - 2, None)
            emit_wdn(4)
        emit_wdn(99)

        bar2 = p.barrier()

        td_toks = list(bar2)
        for f in range(32):
            td_toks.append(p.emit("sync", lambda e, f=f: e.dma_start(
                wT[:, 2048 * f:2048 * (f + 1)], td_hbm[128 * f:128 * (f + 1), :]),
                waits=bar2, inc="LT", amt=16))
        dtok = p.emit("vector", lambda e: e.tensor_scalar(dall[:], am2c[:], sdn_rep2, 1.0 / 127.0,
                                                          OP.mult, OP.mult), waits=bar2, inc="V")

        xbd_ring = Ring([f32(0, D), f32(2048, D)], ["L0", "L1"])
        x1d = f32(4096, D)
        yb = f32(6144, D)
        k2r_ring = Ring([pf16[:, 0:4096], pf16[:, 4096:8192]], ["K0", "K1"])
        k2T_one = pf16[:, 8192:12288]
        psD = [PA[:, 0:1024], PA[:, 1024:2048]]
        psD_users = [list(bar2), list(bar2)]
        psT_users2 = [list(bar2), list(bar2)]
        yb_users = []
        y_last = None
        for i in range(NT):
            slot, ws = xbd_ring.acquire(i)
            ld = p.emit("sync", lambda e, d=slot, i=i: e.dma_start(d[:], x_own[128 * i:128 * (i + 1), :]),
                        waits=ws + bar2, inc=xbd_ring.sem(i), amt=16)
            k2s, k2w = k2r_ring.acquire(i)
            l2 = p.emit("sync", lambda e, d=k2s, i=i: e.dma_start(d[:], k2_hbm[128 * i:128 * (i + 1), :]),
                        waits=k2w + bar2, inc=k2r_ring.sem(i), amt=16)
            k2T, kws = k2T_one, []
            ktr_evs = []
            for g4 in range(8):
                pt = psT[g4 % 2]
                tws = psT_users2[g4 % 2]
                psT_users2[g4 % 2] = []
                mts = []
                for c in range(4):
                    cc = g4 * 4 + c
                    mts.append(p.emit("tensor", lambda e, cc=cc, c=c, pt=pt, k2s=k2s: e.matmul(
                        pt[:, 128 * c:128 * (c + 1)], k2s[:, 128 * cc:128 * (cc + 1)], id16[:],
                        start=True, stop=True),
                        waits=[l2] + tws + kws + (td_toks if i == 0 and g4 == 0 else []), inc="T"))
                ev = p.emit("scalar", lambda e, g4=g4, pt=pt, k2T=k2T: e.activation(
                    k2T[:, 512 * g4:512 * (g4 + 1)], pt[:], AF.Copy), waits=mts, inc="C")
                psT_users2[g4 % 2].append(ev)
                ktr_evs.append(ev)
            k2r_ring.used(i, ktr_evs[-1])
            x1t = p.emit("vector", lambda e, s=slot: e.tensor_tensor(x1d, s, c_rep[:], OP.add),
                         waits=[ld, y_last], inc="V")
            xbd_ring.used(i, x1t)
            for h in range(2):
                pd = psD[h % 2]
                dws = psD_users[h % 2]
                psD_users[h % 2] = []
                mm_last = None
                for c in range(32):
                    for j in range(2):
                        mm_last = p.emit("tensor", lambda e, c=c, j=j, h=h, pd=pd, k2T=k2T: e.matmul(
                            pd[:, 512 * j:512 * (j + 1)],
                            k2T[:, 128 * c:128 * (c + 1)],
                            wT[:, 2048 * c + 1024 * h + 512 * j:2048 * c + 1024 * h + 512 * (j + 1)],
                            start=(c == 0), stop=(c == 31)),
                            waits=(ktr_evs + dws + td_toks) if c == 0 else (), inc="T")
                ov = p.emit("vector", lambda e, h=h, pd=pd, i=i: e.scalar_tensor_tensor(
                    yb[:, 1024 * h:1024 * (h + 1)], pd[:], dall[:, i:i + 1],
                    x1d[:, 1024 * h:1024 * (h + 1)], OP.mult, OP.add),
                    waits=[mm_last, x1t, dtok] + yb_users, inc="V")
                psD_users[h % 2].append(ov)
                yb_users = []
                if h == 1:
                    y_last = p.emit("scalar", lambda e, i=i: e.dma_start(
                        y_out[128 * i:128 * (i + 1), :], yb), waits=[ov], inc="SY", amt=16)
                    yb_users = [y_last]

        final = ("SY", p.cnt["SY"])
        p.emit("scalar", lambda e: e.nop(), waits=[final])
        p.emit("sync", lambda e: e.nop(), waits=[final])

        emit = p.emitters(sems)
        block.sync(emit["sync"])
        block.scalar(emit["scalar"])
        block.vector(emit["vector"])
        block.tensor(emit["tensor"])
        block.gpsimd(emit["gpsimd"])

    return nc


_CACHE = {}


def kernel(L0_attn_x, L0_attn_Wq, L0_attn_bq, L0_attn_Wk, L0_attn_bk,
           L0_attn_Wv, L0_attn_bv, L0_attn_Wo, L0_attn_bo,
           L0_ffn_w_up, L0_ffn_w_down):
    x = np.ascontiguousarray(np.asarray(L0_attn_x, dtype=np.float32)).reshape(16384, D)
    Wv_ = np.ascontiguousarray(np.asarray(L0_attn_Wv, dtype=np.float32))
    Wo_ = np.ascontiguousarray(np.asarray(L0_attn_Wo, dtype=np.float32))
    bv = np.asarray(L0_attn_bv, dtype=np.float32)
    bo = np.asarray(L0_attn_bo, dtype=np.float32)
    wuT = np.ascontiguousarray(np.asarray(L0_ffn_w_up, dtype=np.float32).T)
    wdT = np.ascontiguousarray(np.asarray(L0_ffn_w_down, dtype=np.float32).T)
    bvT = np.ascontiguousarray(bv.reshape(16, 128).T)
    boT = np.ascontiguousarray(bo.reshape(16, 128).T)
    id16 = np.eye(128, dtype=np.float16)
    id32 = np.eye(128, dtype=np.float32)
    ones1 = np.ones((1, 128), dtype=np.float32)
    onec = np.ones((128, 1), dtype=np.float32)

    if "nc" not in _CACHE:
        _CACHE["nc"] = build_nc()
    nc = _CACHE["nc"]

    in_maps = []
    for c in range(8):
        b = c // 2
        half = c % 2
        own = x[b * 4096 + half * 2048: b * 4096 + half * 2048 + 2048]
        sib = x[b * 4096 + (1 - half) * 2048: b * 4096 + (1 - half) * 2048 + 2048]
        in_maps.append({
            "x_own": own, "x_sib": sib, "Wv": Wv_, "Wo": Wo_,
            "wuT": wuT, "wdT": wdT, "bvT": bvT, "boT": boT,
            "id16": id16, "id32": id32, "ones1": ones1, "onec": onec,
        })
    res = run_bass_kernel_spmd(nc, in_maps, core_ids=list(range(8)))
    y = np.empty((16384, D), dtype=np.float32)
    for c in range(8):
        b = c // 2
        half = c % 2
        y[b * 4096 + half * 2048: b * 4096 + half * 2048 + 2048] = res.results[c]["y"]
    return y.reshape(4, 4096, D)


# version 22
# speedup vs baseline: 1.0089x; 1.0089x over previous
import numpy as np
import concourse.bass as bass
import concourse.mybir as mybir
from concourse.bass_utils import run_bass_kernel_spmd

F32 = mybir.dt.float32
F16 = mybir.dt.float16
F8 = mybir.dt.float8e4
I32 = mybir.dt.int32
OP = mybir.AluOpType
AF = mybir.ActivationFunctionType
AX = mybir.AxisListType.X

D = 2048
FF = 4096
TOK = 2048
NT = 16
EPS = 1e-5

ENGS = ("sync", "scalar", "vector", "tensor", "gpsimd")
DEBUG = False


class Prog:
    def __init__(self):
        self.ops = {e: [] for e in ENGS}
        self.cnt = {}

    def ch(self, name):
        self.cnt[name] = 0

    ENG_CH = {"vector": "V", "scalar": "C", "gpsimd": "G"}

    def emit(self, eng, fn, waits=(), inc=None, amt=1):
        if inc is None and eng in self.ENG_CH:
            inc = self.ENG_CH[eng]
            amt = 1
        tok = None
        if inc is not None:
            self.cnt[inc] += amt
            tok = (inc, self.cnt[inc])
        ws = tuple(w for w in waits if w is not None)
        self.ops[eng].append((fn, ws, inc, amt))
        return tok

    def barrier(self):
        return [(c, n) for c, n in self.cnt.items()]

    def emitters(self, sems):
        last = {}
        COMPUTE = ("V", "C", "G")

        def mk(name):
            def run(eng):
                pending = None
                for fn, waits, inc, amt in self.ops[name]:
                    if pending is not None:
                        s, v = pending
                        k = (name, s)
                        if last.get(k, 0) < v:
                            eng.wait_ge(sems[s], v)
                            last[k] = v
                        pending = None
                    for (s, v) in waits:
                        k = (name, s)
                        if last.get(k, 0) < v:
                            eng.wait_ge(sems[s], v)
                            last[k] = v
                    inst = fn(eng)
                    if inc is not None:
                        inst.then_inc(sems[inc], amt)
                        if inc in COMPUTE:
                            self_tok = None
                            pending = (inc, self._emitted_cnt[name][inc])
                for _ in ():
                    pass
            return run
        self._emitted_cnt = {e: {} for e in ENGS}
        for e in ENGS:
            cnt = {}
            vals = []
            for fn, waits, inc, amt in self.ops[e]:
                if inc is not None:
                    cnt[inc] = cnt.get(inc, 0) + amt
                    vals.append(cnt[inc])
                else:
                    vals.append(None)
            self._emitted_cnt[e] = (cnt, vals)

        def mk2(name):
            _, vals = self._emitted_cnt[name]

            def run(eng):
                pending = None
                for idx, (fn, waits, inc, amt) in enumerate(self.ops[name]):
                    if pending is not None:
                        s, v = pending
                        k = (name, s)
                        if last.get(k, 0) < v:
                            eng.wait_ge(sems[s], v)
                            last[k] = v
                        pending = None
                    for (s, v) in waits:
                        k = (name, s)
                        if last.get(k, 0) < v:
                            eng.wait_ge(sems[s], v)
                            last[k] = v
                    inst = fn(eng)
                    if inc is not None:
                        inst.then_inc(sems[inc], amt)
                        if inc in COMPUTE:
                            pending = (inc, vals[idx])
            return run
        return {e: mk2(e) for e in ENGS}


class Ring:
    def __init__(self, slots, semnames=None):
        self.slots = list(slots)
        self.n = len(self.slots)
        self.users = [[] for _ in self.slots]
        self.sems = semnames or [None] * self.n

    def acquire(self, i):
        s = i % self.n
        waits = self.users[s]
        self.users[s] = []
        return self.slots[s], waits

    def sem(self, i):
        return self.sems[i % self.n]

    def used(self, i, tok):
        if tok is not None:
            self.users[i % self.n].append(tok)


def build_nc():
    nc = bass.Bass("TRN2", target_bir_lowering=False, detect_race_conditions=False)

    x_own = nc.dram_tensor("x_own", [TOK, D], F32, kind="ExternalInput").ap()
    x_sib = nc.dram_tensor("x_sib", [TOK, D], F32, kind="ExternalInput").ap()
    Wv = nc.dram_tensor("Wv", [D, D], F32, kind="ExternalInput").ap()
    Wo = nc.dram_tensor("Wo", [D, D], F32, kind="ExternalInput").ap()
    wuT = nc.dram_tensor("wuT", [D, FF], F32, kind="ExternalInput").ap()
    wdT = nc.dram_tensor("wdT", [FF, D], F32, kind="ExternalInput").ap()
    bvT_h = nc.dram_tensor("bvT", [128, 16], F32, kind="ExternalInput").ap()
    boT_h = nc.dram_tensor("boT", [128, 16], F32, kind="ExternalInput").ap()
    id16_h = nc.dram_tensor("id16", [128, 128], F16, kind="ExternalInput").ap()
    id32_h = nc.dram_tensor("id32", [128, 128], F32, kind="ExternalInput").ap()
    ones1_h = nc.dram_tensor("ones1", [1, 128], F32, kind="ExternalInput").ap()
    onec_h = nc.dram_tensor("onec", [128, 1], F32, kind="ExternalInput").ap()
    y_out = nc.dram_tensor("y", [TOK, D], F32, kind="ExternalOutput").ap()
    k2_hbm = nc.dram_tensor("k2s", [TOK, FF], F16, kind="Internal").ap()
    td_hbm = nc.dram_tensor("tds", [FF, D], F8, kind="Internal").ap()
    us_hbm = nc.dram_tensor("us", [16, 128], F32, kind="Internal").ap()
    if DEBUG:
        dbg_crep = nc.dram_tensor("dbg_crep", [128, D], F32, kind="ExternalOutput").ap()
        dbg_srep = nc.dram_tensor("dbg_srep", [128, 8], F32, kind="ExternalOutput").ap()
        dbg_kf = nc.dram_tensor("dbg_kf", [128, D], F16, kind="ExternalOutput").ap()
        dbg_hb = nc.dram_tensor("dbg_hb", [128, FF], F32, kind="ExternalOutput").ap()
        dbg_am2 = nc.dram_tensor("dbg_am2", [128, 16], F32, kind="ExternalOutput").ap()
        dbg_wt = nc.dram_tensor("dbg_wt", [128, 4096], F8, kind="ExternalOutput").ap()
        dbg_x1 = nc.dram_tensor("dbg_x1", [128, D], F32, kind="ExternalOutput").ap()
        dbg_tv = nc.dram_tensor("dbg_tv", [128, 64], F32, kind="ExternalOutput").ap()
        dbg_xrow = nc.dram_tensor("dbg_xrow", [1, D], F32, kind="ExternalOutput").ap()
        dbg_rep = nc.dram_tensor("dbg_rep", [128, D], F32, kind="ExternalOutput").ap()
        dbg_ucol = nc.dram_tensor("dbg_ucol", [128, 16], F32, kind="ExternalOutput").ap()
        dbg_uT = nc.dram_tensor("dbg_uT", [16, 128], F32, kind="ExternalOutput").ap()
        dbg_m1 = nc.dram_tensor("dbg_m1", [128, D], F16, kind="ExternalOutput").ap()
    cs_hbm = nc.dram_tensor("cs", [16, 128], F32, kind="Internal").ap()

    p = Prog()
    DMACH = ("LC", "L0", "L1", "W0", "W1", "LB", "LT", "K0", "K1",
             "U0", "U1", "U2", "U3", "L2", "L3", "D0", "D1", "SB", "SY", "SK", "SD0", "SD1", "SG")
    for c in ("T", "V", "C", "G") + DMACH:
        p.ch(c)

    from contextlib import ExitStack
    es = ExitStack()
    with es:
        id16 = es.enter_context(nc.sbuf_tensor("sb_id16", [128, 128], F16))
        id32 = es.enter_context(nc.sbuf_tensor("sb_id32", [128, 128], F32))
        ones1 = es.enter_context(nc.sbuf_tensor("sb_ones1", [1, 128], F32))
        onec = es.enter_context(nc.sbuf_tensor("sb_onec", [128, 1], F32))
        bvT = es.enter_context(nc.sbuf_tensor("sb_bvT", [128, 16], F32))
        boT = es.enter_context(nc.sbuf_tensor("sb_boT", [128, 16], F32))
        c_rep = es.enter_context(nc.sbuf_tensor("c_rep", [128, D], F32))
        am2c = es.enter_context(nc.sbuf_tensor("am2c", [128, 16], F32))
        wabs_u = es.enter_context(nc.sbuf_tensor("wabs_u", [128, 32], F32))
        wabs_d = es.enter_context(nc.sbuf_tensor("wabs_d", [128, 32], F32))
        ucol = es.enter_context(nc.sbuf_tensor("ucol", [128, 16], F32))
        ccol = es.enter_context(nc.sbuf_tensor("ccol", [128, 16], F32))
        uTsb = es.enter_context(nc.sbuf_tensor("uT", [16, 128], F32))
        cTsb = es.enter_context(nc.sbuf_tensor("cT", [16, 128], F32))
        tiny = es.enter_context(nc.sbuf_tensor("tiny", [128, 64], F32))
        tinyC = es.enter_context(nc.sbuf_tensor("tinyC", [128, 16], F32))
        xsall = es.enter_context(nc.sbuf_tensor("xsall", [128, 32], F32))
        ssqall = es.enter_context(nc.sbuf_tensor("ssqall", [128, 32], F32))
        aall = es.enter_context(nc.sbuf_tensor("aall", [128, 32], F32))
        maall = es.enter_context(nc.sbuf_tensor("maall", [128, 32], F32))
        p1s = es.enter_context(nc.sbuf_tensor("p1s", [128, 128], F32))
        srep = es.enter_context(nc.sbuf_tensor("srep", [128, 8], F32))
        dall = es.enter_context(nc.sbuf_tensor("dall", [128, 16], F32))
        t8a = es.enter_context(nc.sbuf_tensor("t8a", [128, 2048], F8))
        mkq = es.enter_context(nc.sbuf_tensor("mkq8", [128, 1024], F8))
        mk8 = es.enter_context(nc.sbuf_tensor("mk88", [128, 1024], F8))
        t8b = es.enter_context(nc.sbuf_tensor("t8b", [128, 2048], F8))
        pf32 = es.enter_context(nc.sbuf_tensor("pf32", [128, 23680], F32))
        pf16 = es.enter_context(nc.sbuf_tensor("pf16", [128, 12288], F16))
        pi32 = es.enter_context(nc.sbuf_tensor("pi32", [128, 2048], I32))
        wT = es.enter_context(nc.sbuf_tensor("wT", [128, 65536], F8))
        PA = es.enter_context(nc.psum_tensor("PA", [128, 2048], F32))
        PM = es.enter_context(nc.psum_tensor("PM", [128, 512], F32))
        PT2 = es.enter_context(nc.psum_tensor("PT2", [128, 512], F32))
        PS = es.enter_context(nc.psum_tensor("PS", [128, 512], F32))
        sems = {}
        for c in ("T", "V", "C", "G") + DMACH:
            sems[c] = es.enter_context(nc.semaphore("sem" + c))
        block = es.enter_context(nc.Block())
        def f32(off, n):
            return pf32[:, off:off + n]

        TC = {}

        def tc(name):
            if name not in TC:
                TC[name] = len(TC)
                assert len(TC) <= 16
            i = TC[name]
            return tinyC[:, i:i + 1]

        TV = {}

        def tv(name):
            if name not in TV:
                TV[name] = len(TV)
                assert len(TV) <= 56
            i = TV[name]
            return tiny[:, i:i + 1]

        onesm = f32(23552, 128)
        om = p.emit("vector", lambda e: e.memset(onesm, 1.0), inc="V")
        for dst, src in ((id16, id16_h), (id32, id32_h), (ones1, ones1_h),
                         (onec, onec_h), (bvT, bvT_h), (boT, boT_h)):
            p.emit("sync", lambda e, d=dst, s=src: e.dma_start(d[:], s[:]), inc="LC", amt=16)
        tok_const = ("LC", p.cnt["LC"])

        xb_ring = Ring([f32(0, D), f32(2048, D), f32(4096, D)], ["L0", "L1", "L2"])
        wu_ring = Ring([f32(6144, 4096), f32(10240, 4096)], ["U0", "U1"])
        wdnB_ring = Ring([f32(14336, 4096), f32(18432, 4096)], ["D0", "D1"])
        cdump = f32(12288, D)
        vdump = f32(14336, D)

        last_mm = None
        last_bs = None

        wdT2 = wdT.rearrange("(a b) c -> a (b c)", b=2)
        td2 = td_hbm.rearrange("(a b) c -> a (b c)", b=2)
        bnb = p1s[:, 64:88]
        wu_i = [0]
        wd_i = [0]
        absu_last = [None]
        absd_last2 = [None]

        def emit_absu(r):
            i = wu_i[0]; wu_i[0] += 1
            slot, ws = wu_ring.acquire(i)
            ld = p.emit("scalar", lambda e, d=slot, r=r: e.dma_start(
                d[:], wuT[128 * r:128 * (r + 1), :]), waits=ws, inc=wu_ring.sem(i), amt=16)
            p.emit("scalar", lambda e, s=slot, r=r: e.activation(
                s[:, 0:2048], s[:, 0:2048], AF.Abs, accum_out=wabs_u[:, 2 * r:2 * r + 1]),
                waits=[ld], inc="C")
            a2 = p.emit("scalar", lambda e, s=slot, r=r: e.activation(
                s[:, 2048:4096], s[:, 2048:4096], AF.Abs, accum_out=wabs_u[:, 2 * r + 1:2 * r + 2]),
                inc="C")
            wu_ring.used(i, a2)
            absu_last[0] = a2

        def emit_absd(r):
            i = wd_i[0]; wd_i[0] += 1
            slot, ws = wdnB_ring.acquire(i)
            ld = p.emit("gpsimd", lambda e, d=slot, r=r: e.dma_start(
                d[:], wdT2[128 * r:128 * (r + 1), :]), waits=ws, inc=wdnB_ring.sem(i), amt=16)
            p.emit("scalar", lambda e, s=slot, r=r: e.activation(
                s[:, 0:2048], s[:, 0:2048], AF.Abs, accum_out=wabs_d[:, 2 * r:2 * r + 1]),
                waits=[ld], inc="C")
            a2 = p.emit("scalar", lambda e, s=slot, r=r: e.activation(
                s[:, 2048:4096], s[:, 2048:4096], AF.Abs, accum_out=wabs_d[:, 2 * r + 1:2 * r + 2]),
                inc="C")
            wdnB_ring.used(i, a2)
            absd_last2[0] = a2

        ma_last = None
        for t in range(32):
            src_t = x_own if t < 16 else x_sib
            r0 = (t % 16) * 128
            slot, ws = xb_ring.acquire(t)
            ld = p.emit("sync", lambda e, d=slot, s=src_t, r=r0: e.dma_start(d[:], s[r:r + 128, :]),
                        waits=ws + [tok_const], inc=xb_ring.sem(t), amt=16)
            for q in range(4):
                p.emit("vector", lambda e, xbv=slot, q=q: e.bn_stats(
                    bnb[:, 6 * q:6 * (q + 1)], xbv[:, 512 * q:512 * (q + 1)]),
                    waits=[ld], inc="V")
            mvt = p1s[:, 2 * t:2 * t + 2]
            ag = p.emit("vector", lambda e, mvt=mvt: e.bn_aggr(mvt, bnb), inc="V")
            p.emit("vector", lambda e, mvt=mvt: e.tensor_scalar(
                tv("ve"), mvt[:, 1:2], 1.0, EPS, OP.mult, OP.add))
            rv = p.emit("vector", lambda e: e.reciprocal(tv("rq"), tv("ve")), inc="V")
            rc = p.emit("scalar", lambda e, t=t: e.sqrt(aall[:, t:t + 1], tv("rq")),
                        waits=[rv], inc="C")
            ma_last = p.emit("vector", lambda e, mvt=mvt, t=t: e.tensor_tensor(
                maall[:, t:t + 1], mvt[:, 0:1], aall[:, t:t + 1], OP.mult),
                waits=[rc], inc="V")
            for q in range(4):
                last_mm = p.emit("tensor", lambda e, xbv=slot, q=q, t=t,
                                 s0=(t == 0), s1=(t == 31): e.matmul(
                    PA[0:1, 512 * q:512 * (q + 1)], aall[:, t:t + 1],
                    xbv[:, 512 * q:512 * (q + 1)], start=s0, stop=s1),
                    waits=[rc, ld], inc="T")
                xb_ring.used(t, last_mm)
            xb_ring.used(t, ag)
            if t % 2 == 0:
                emit_absu(t // 2)
            else:
                emit_absd(t // 2)

        svd = p.emit("vector", lambda e: e.tensor_reduce(tv("wsd"), wabs_d[:], AX, OP.add),
                     waits=[absd_last2[0]], inc="V")
        tkd = p.emit("tensor", lambda e: e.matmul(PS[:, 17:18], onesm, tv("wsd"),
                                                  start=True, stop=True), waits=[svd, om], inc="T")
        rrd = p.emit("vector", lambda e: e.tensor_copy(tv("draw"), PS[:, 17:18]),
                     waits=[tkd], inc="V")
        p.emit("vector", lambda e: e.tensor_scalar(srep[:, 2:3], tv("draw"),
                                                   1.0 / (D * FF), EPS, OP.mult, OP.max))
        sdn_tok = p.emit("vector", lambda e: e.reciprocal(srep[:, 3:4], srep[:, 2:3]), inc="V")
        thd_tok = p.emit("vector", lambda e: e.tensor_scalar(
            srep[:, 5:6], srep[:, 2:3], 0.5, None, OP.mult), inc="V")
        sdn_rep2 = srep[:, 2:3]
        ivd_rep2 = srep[:, 3:4]
        thd_rep = srep[:, 5:6]

        sv = p.emit("vector", lambda e: e.tensor_reduce(tv("wsu"), wabs_u[:], AX, OP.add),
                    waits=[absu_last[0]], inc="V")
        tksu = p.emit("tensor", lambda e: e.matmul(PS[:, 32:33], onesm, tv("wsu"),
                                                   start=True, stop=True),
                      waits=[sv, om], inc="T")
        rr = p.emit("vector", lambda e: e.tensor_copy(tv("wraw"), PS[:, 32:33]),
                    waits=[tksu], inc="V")
        p.emit("vector", lambda e: e.tensor_scalar(srep[:, 0:1], tv("wraw"),
                                                   1.0 / (D * FF), EPS, OP.mult, OP.max))
        bu = p.emit("vector", lambda e: e.reciprocal(srep[:, 1:2], srep[:, 0:1]), inc="V")
        thu_tok = p.emit("vector", lambda e: e.tensor_scalar(
            srep[:, 4:5], srep[:, 0:1], 0.5, None, OP.mult), inc="V")
        sup_rep = srep[:, 0:1]
        ivu_rep = srep[:, 1:2]
        thu_rep = srep[:, 4:5]

        last_bs = p.emit("tensor", lambda e: e.matmul(PS[:, 0:32], onesm, maall[:],
                                                      start=True, stop=True),
                         waits=[rr, om], inc="T")

        m1b = pf16[:, 0:4096]
        m2b = pf16[:, 4096:8192]
        t8_rP = Ring([t8a, t8b], ["SD0", "SD1"])
        ternV_last = None
        ternd_P_toks = []
        for r in range(16):
            i = wu_i[0]; wu_i[0] += 1
            slot, ws = wu_ring.acquire(i)
            ld = p.emit("scalar", lambda e, d=slot, r=r: e.dma_start(
                d[:], wuT[128 * r:128 * (r + 1), :]), waits=ws + [thu_tok], inc=wu_ring.sem(i), amt=16)
            p.emit("vector", lambda e, s=slot: e.tensor_scalar(
                m1b, s, -1.0, thu_rep, OP.mult, OP.is_gt), waits=[ld, thu_tok])
            ternV_last = p.emit("vector", lambda e, s=slot, r=r: e.scalar_tensor_tensor(
                wT[:, 4096 * r:4096 * (r + 1)], s, thu_rep, m1b, OP.is_gt, OP.subtract), inc="V")
            wu_ring.used(i, ternV_last)

            i = wd_i[0]; wd_i[0] += 1
            slot, ws = wdnB_ring.acquire(i)
            ld = p.emit("gpsimd", lambda e, d=slot, r=r: e.dma_start(
                d[:], wdT2[128 * r:128 * (r + 1), :]), waits=ws + [thd_tok],
                inc=wdnB_ring.sem(i), amt=16)
            p.emit("vector", lambda e, s=slot: e.tensor_scalar(
                m1b, s, -1.0, thd_rep, OP.mult, OP.is_gt), waits=[ld, thd_tok])
            ta, taw = t8_rP.acquire(2 * r)
            t1 = p.emit("vector", lambda e, s=slot, o=ta: e.scalar_tensor_tensor(
                o[:], s[:, 0:2048], thd_rep, m1b[:, 0:2048], OP.is_gt, OP.subtract),
                waits=taw, inc="V")
            sa = p.emit("sync", lambda e, r=r, o=ta: e.dma_start(
                td2[128 * r:128 * (r + 1), 0:2048], o[:]), waits=[t1],
                inc=t8_rP.sem(2 * r), amt=16)
            t8_rP.used(2 * r, sa)
            tb, tbw = t8_rP.acquire(2 * r + 1)
            t2 = p.emit("vector", lambda e, s=slot, o=tb: e.scalar_tensor_tensor(
                o[:], s[:, 2048:4096], thd_rep, m1b[:, 2048:4096], OP.is_gt, OP.subtract),
                waits=tbw, inc="V")
            sb = p.emit("sync", lambda e, r=r, o=tb: e.dma_start(
                td2[128 * r:128 * (r + 1), 2048:4096], o[:]), waits=[t2],
                inc=t8_rP.sem(2 * r + 1), amt=16)
            t8_rP.used(2 * r + 1, sb)
            wdnB_ring.used(i, t2)
            ternV_last = t2
            ternd_P_toks.append(sb)

        xrow = pf32[0:1, 18432:18432 + D]
        e1 = p.emit("vector", lambda e: e.tensor_copy(xrow, PA[0:1, :]), waits=[last_mm], inc="V")
        bs = p.emit("vector", lambda e: e.tensor_reduce(tv("bsr"), PS[:, 0:32], AX, OP.add),
                    waits=[last_bs], inc="V")
        rep = f32(16384, D)
        brd = []
        for q in range(4):
            brd.append(p.emit("tensor", lambda e, q=q: e.matmul(
                PA[:, 512 * q:512 * (q + 1)], ones1[:], xrow[0:1, 512 * q:512 * (q + 1)],
                start=True, stop=True), waits=[e1], inc="T"))
        re1 = p.emit("vector", lambda e: e.tensor_scalar(rep, PA[:], tv("bsr"), 1.0 / 4096.0,
                                                         OP.subtract, OP.mult),
                     waits=brd + [bs], inc="V")
        if DEBUG:
            p.emit("scalar", lambda e: e.dma_start(dbg_xrow[:], rep[0:1, :]), waits=[re1], inc="SG", amt=16)

        if DEBUG:
            p.emit("scalar", lambda e: e.dma_start(dbg_rep[:], rep), waits=[re1], inc="SG", amt=16)
        wvo_ring = Ring([f32(8192, D), f32(10240, D)], ["W0", "W1"])
        ttr_last = None
        for t in range(16):
            slot, ws = wvo_ring.acquire(t)
            ld = p.emit("sync", lambda e, d=slot, t=t: e.dma_start(d[:], Wv[128 * t:128 * (t + 1), :]),
                        waits=ws, inc=wvo_ring.sem(t), amt=16)
            ttr_last = p.emit("vector", lambda e, s=slot, t=t: e.scalar_tensor_tensor(
                vdump, s, 1.0, rep, OP.mult, OP.mult, accum_out=ucol[:, t:t + 1]),
                waits=[ld, re1], inc="V")
            wvo_ring.used(t, ttr_last)
        ub = p.emit("vector", lambda e: e.tensor_tensor(ucol[:], ucol[:], bvT[:], OP.add),
                    waits=[ttr_last], inc="V")
        if DEBUG:
            p.emit("scalar", lambda e: e.dma_start(dbg_ucol[:], ucol[:]), waits=[ub], inc="SG", amt=16)
        tk = p.emit("tensor", lambda e: e.matmul(PM[0:16, 0:128], ucol[:], id32[:],
                                                 start=True, stop=True), waits=[ub], inc="T")
        ue = p.emit("vector", lambda e: e.tensor_copy(uTsb[:], PM[0:16, 0:128]), waits=[tk], inc="V")
        if DEBUG:
            p.emit("scalar", lambda e: e.dma_start(dbg_uT[:], uTsb[:]), waits=[ue], inc="SG", amt=16)
        ust = p.emit("scalar", lambda e: e.dma_start(us_hbm[:], uTsb[:]), waits=[ue], inc="SB", amt=16)
        uld = p.emit("sync", lambda e: e.dma_start(
            xrow, us_hbm.rearrange("a b -> (a b)")[None, :]), waits=[ust, e1] + brd, inc="LB", amt=16)
        brd2 = []
        for q in range(4):
            brd2.append(p.emit("tensor", lambda e, q=q: e.matmul(
                PA[:, 512 * q:512 * (q + 1)], ones1[:], xrow[0:1, 512 * q:512 * (q + 1)],
                start=True, stop=True), waits=[uld, re1], inc="T"))
        re2 = p.emit("scalar", lambda e: e.activation(rep, PA[:], AF.Copy),
                     waits=brd2 + [ttr_last], inc="C")

        tto_last = None
        for t in range(16):
            slot, ws = wvo_ring.acquire(16 + t)
            ld = p.emit("sync", lambda e, d=slot, t=t: e.dma_start(d[:], Wo[128 * t:128 * (t + 1), :]),
                        waits=ws, inc=wvo_ring.sem(16 + t), amt=16)
            tto_last = p.emit("vector", lambda e, s=slot, t=t: e.scalar_tensor_tensor(
                vdump, s, 1.0, rep, OP.mult, OP.mult, accum_out=ccol[:, t:t + 1]),
                waits=[ld, re2], inc="V")
            wvo_ring.used(16 + t, tto_last)
        cb = p.emit("vector", lambda e: e.tensor_tensor(ccol[:], ccol[:], boT[:], OP.add),
                    waits=[tto_last], inc="V")
        tk = p.emit("tensor", lambda e: e.matmul(PM[0:16, 0:128], ccol[:], id32[:],
                                                 start=True, stop=True), waits=[cb, ue], inc="T")
        ce = p.emit("vector", lambda e: e.tensor_copy(cTsb[:], PM[0:16, 0:128]), waits=[tk], inc="V")
        cst = p.emit("scalar", lambda e: e.dma_start(cs_hbm[:], cTsb[:]), waits=[ce], inc="SB", amt=16)
        cld = p.emit("sync", lambda e: e.dma_start(
            xrow, cs_hbm.rearrange("a b -> (a b)")[None, :]), waits=[cst, uld] + brd2, inc="LB", amt=16)
        brd3 = []
        for q in range(4):
            brd3.append(p.emit("tensor", lambda e, q=q: e.matmul(
                PA[:, 512 * q:512 * (q + 1)], ones1[:], xrow[0:1, 512 * q:512 * (q + 1)],
                start=True, stop=True), waits=[cld, re2], inc="T"))
        crep_tok = p.emit("scalar", lambda e: e.activation(c_rep[:], PA[:], AF.Copy),
                          waits=brd3, inc="C")

        bar = p.barrier()

        xbu_ring = Ring([f32(0, D), f32(2048, D)], ["L0", "L1"])
        x1b = f32(4096, D)
        hb = f32(8192, FF)
        zq = f32(18432, 1024)
        mq = f32(19456, 1024)
        e8 = f32(20480, 1024)
        z8 = f32(21504, 1024)
        s8 = f32(22528, 1024)
        wdn_ring = Ring([f32(14336, D), f32(16384, D)], ["D0", "D1"])
        kf = pf16[:, 0:2048]
        dm1 = pf16[:, 2048:4096]
        dm2 = pf16[:, 4096:6144]
        kT_one = pf16[:, 6144:8192]
        k2b = pf16[:, 8192:12288]
        ziq = pi32[:, 0:1024]
        zi8 = pi32[:, 1024:2048]

        psU = [PA[:, 0:1024], PA[:, 1024:2048]]
        psU_users = [[crep_tok], [crep_tok]]
        psT = [PM, PT2]
        psT_users = [[ce, bu], [ce, bu]]

        wdn_items = []
        wdn_iter = iter(wdn_items)
        wdn_i = [0]
        absd_last = [None]
        sdn_done = [None]
        ternd_toks = []
        t8_ring = Ring([t8a, t8b], ["SD0", "SD1"])
        sdn_rep = srep[:, 2:3]
        ivd_rep = srep[:, 3:4]
        dm_prev = [None]

        def emit_wdn(n):
            for _ in range(n):
                it = next(wdn_iter, None)
                if it is None:
                    return
                kind, r = it
                if kind == "sdn":
                    sv = p.emit("vector", lambda e: e.tensor_reduce(tv("wsd"), wabs_d[:], AX, OP.add),
                                waits=[absd_last2[0]] + bar, inc="V")
                    tk = p.emit("tensor", lambda e: e.matmul(PS[:, 17:18], onesm, tv("wsd"),
                                                             start=True, stop=True),
                                waits=[sv] + bar, inc="T")
                    rr = p.emit("vector", lambda e: e.tensor_copy(tv("draw"), PS[:, 17:18]),
                                waits=[tk], inc="V")
                    p.emit("vector", lambda e: e.tensor_scalar(srep[:, 2:3], tv("draw"),
                                                               1.0 / (D * FF), EPS, OP.mult, OP.max))
                    sdn_done[0] = p.emit("vector", lambda e: e.reciprocal(srep[:, 3:4], srep[:, 2:3]),
                                         inc="V")
                else:
                    i = wdn_i[0]; wdn_i[0] += 1
                    slot, ws = wdn_ring.acquire(i)
                    ld = p.emit("sync", lambda e, d=slot, r=r: e.dma_start(
                        d[:], wdT[128 * r:128 * (r + 1), :]), waits=ws + [sdn_done[0]],
                        inc=wdn_ring.sem(i), amt=16)
                    g1 = p.emit("vector", lambda e, s=slot: e.tensor_scalar(
                        dm1, s, ivd_rep, 0.5, OP.mult, OP.is_gt),
                        waits=[ld, sdn_done[0], dm_prev[0]])
                    g2 = p.emit("vector", lambda e, s=slot: e.tensor_scalar(
                        dm2, s, ivd_rep, -0.5, OP.mult, OP.is_lt), waits=[ld])
                    t8, t8w = t8_ring.acquire(len(ternd_toks))
                    tt = p.emit("vector", lambda e, o=t8: e.tensor_tensor(o[:], dm1, dm2, OP.subtract),
                                waits=[g2, g1] + t8w, inc="V")
                    dm_prev[0] = tt
                    st = p.emit("scalar", lambda e, r=r, o=t8: e.dma_start(
                        td_hbm[128 * r:128 * (r + 1), :], o[:]), waits=[tt],
                        inc=t8_ring.sem(len(ternd_toks)), amt=16)
                    t8_ring.used(len(ternd_toks), st)
                    wdn_ring.used(i, tt)
                    ternd_toks.append(st)

        prev_sq = [None]
        kf_free = [None]
        k2b_free = [None]
        st_state = {}

        front_nb = [None]

        def emit_front(i):
            slot, ws = xbu_ring.acquire(i)
            ld = p.emit("sync", lambda e, d=slot, i=i: e.dma_start(d[:], x_own[128 * i:128 * (i + 1), :]),
                        waits=ws + bar, inc=xbu_ring.sem(i), amt=16)
            x1 = p.emit("vector", lambda e, s=slot: e.scalar_tensor_tensor(
                x1b, s, 0.0, c_rep[:], OP.add, OP.add, accum_out=tv("xs")),
                waits=[ld, crep_tok, prev_sq[0]] + bar, inc="V")
            xbu_ring.used(i, x1)
            p.emit("vector", lambda e: e.tensor_reduce(tv("rmax"), x1b, AX, OP.max))
            p.emit("vector", lambda e: e.tensor_reduce(tv("rmin"), x1b, AX, OP.min))
            prev_sq[0] = p.emit("scalar", lambda e: e.activation(
                cdump, x1b, AF.Square, accum_out=tc("ssq")), waits=[x1] + bar, inc="C")
            p.emit("vector", lambda e: e.tensor_scalar(tv("mean"), tv("xs"), 1.0 / D, None, OP.mult))
            p.emit("vector", lambda e: e.tensor_tensor(tv("me2"), tv("mean"), tv("mean"), OP.mult))
            p.emit("vector", lambda e: e.tensor_scalar(tv("v1"), tc("ssq"), 1.0 / D, None, OP.mult),
                   waits=[prev_sq[0]])
            p.emit("vector", lambda e: e.tensor_tensor(tv("var"), tv("v1"), tv("me2"), OP.subtract))
            p.emit("vector", lambda e: e.tensor_scalar(tv("ve"), tv("var"), EPS, None, OP.add))
            rv = p.emit("vector", lambda e: e.reciprocal(tv("rq"), tv("ve")), inc="V")
            rs = p.emit("scalar", lambda e: e.sqrt(tc("rstd"), tv("rq")), waits=[rv], inc="C")
            p.emit("vector", lambda e: e.tensor_tensor(tv("t1"), tv("rmax"), tv("mean"), OP.subtract),
                   waits=[rs])
            p.emit("vector", lambda e: e.tensor_tensor(tv("t2"), tv("mean"), tv("rmin"), OP.subtract))
            p.emit("vector", lambda e: e.tensor_tensor(tv("t3"), tv("t1"), tv("t2"), OP.max))
            p.emit("vector", lambda e: e.tensor_tensor(tv("amax"), tv("t3"), tc("rstd"), OP.mult))
            amc = tv("amc%d" % (i % 2))
            front_amc = amc
            p.emit("vector", lambda e, amc=amc: e.tensor_scalar(amc, tv("amax"), EPS, None, OP.max))
            p.emit("vector", lambda e, amc=amc: e.tensor_scalar(tv("r127"), amc, 1.0 / 127.0, None, OP.mult))
            p.emit("vector", lambda e: e.reciprocal(tv("scl"), tv("r127")))
            p.emit("vector", lambda e: e.tensor_tensor(tv("sa"), tc("rstd"), tv("scl"), OP.mult))
            p.emit("vector", lambda e: e.tensor_tensor(tv("nb0"), tv("mean"), tv("sa"), OP.mult))
            nb = p.emit("vector", lambda e: e.tensor_scalar(tv("nbias"), tv("nb0"), -1.0, None, OP.mult),
                        inc="V")
            front_nb[0] = nb

        def emit_front_b(i):
            nb = front_nb[0]
            kq_last = None
            zcs = {}

            def q_zv(q):
                s_ = q % 2
                bsl = slice(512 * s_, 512 * (s_ + 1))
                zh = zq[:, bsl]
                sl = slice(512 * q, 512 * (q + 1))
                zv = p.emit("vector", lambda e, sl=sl, zh=zh: e.tensor_scalar(
                    zh, x1b[:, sl], tv("sa"), tv("nbias"), OP.mult, OP.add),
                    waits=[nb] + ([kf_free[0]] if q <= 1 else []), inc="V")
                zcs[q] = p.emit("scalar", lambda e, zh=zh, zih=ziq[:, bsl]: e.activation(
                    zih, zh, AF.Copy), waits=[zv] + bar, inc="C")

            def q_fix(q):
                s_ = q % 2
                bsl = slice(512 * s_, 512 * (s_ + 1))
                zh = zq[:, bsl]; mh = mq[:, bsl]; zih = ziq[:, bsl]
                sl = slice(512 * q, 512 * (q + 1))
                p.emit("vector", lambda e, zh=zh, zih=zih, mh2=mh: e.tensor_tensor(
                    mh2, zh, zih, OP.subtract), waits=[zcs[q]])
                p.emit("vector", lambda e, mh2=mh, zih=zih, zh=zh: e.scalar_tensor_tensor(
                    zh, mh2, -0.5, zih, OP.is_lt, OP.subtract))
                return p.emit("vector", lambda e, sl=sl, mh2=mh, zh=zh: e.scalar_tensor_tensor(
                    kf[:, sl], mh2, 0.5, zh, OP.is_gt, OP.subtract), inc="V")

            q_zv(0); q_zv(1)
            q_fix(0)
            q_zv(2)
            q_fix(1)
            q_zv(3)
            q_fix(2)
            kq_last = q_fix(3)
            kT, kws = kT_one, []
            ktr_evs = []
            tr_last = None
            for g4 in range(4):
                pt = psT[g4 % 2]
                tws = psT_users[g4 % 2]
                psT_users[g4 % 2] = []
                mts = []
                for c in range(4):
                    cc = g4 * 4 + c
                    tr_last = p.emit("tensor", lambda e, cc=cc, c=c, pt=pt: e.matmul(
                        pt[:, 128 * c:128 * (c + 1)], kf[:, 128 * cc:128 * (cc + 1)], id16[:],
                        start=True, stop=True),
                        waits=[kq_last] + tws + kws + bar, inc="T")
                    mts.append(tr_last)
                ev = p.emit("scalar", lambda e, g4=g4, pt=pt, kT=kT: e.activation(
                    kT[:, 512 * g4:512 * (g4 + 1)], pt[:], AF.Copy), waits=mts, inc="C")
                psT_users[g4 % 2].append(ev)
                ktr_evs.append(ev)
            kf_free[0] = tr_last
            st_state[i] = {"kT": kT, "ktr": ktr_evs, "amc": tv("amc%d" % (i % 2))}

        def emit_gamma(i):
            st = st_state[i]
            sg_prev = st_state.get(i - 1, {}).get("sg")
            p.emit("vector", lambda e: e.tensor_scalar(tv("Mr"), tv("Mq0"), 0.0, None, OP.max),
                   waits=st["mr"] + ([sg_prev] if sg_prev else []))
            p.emit("vector", lambda e, amc=st["amc"]: e.tensor_scalar(
                tv("ba"), amc, sup_rep, 1.0 / 127.0, OP.mult, OP.mult))
            p.emit("vector", lambda e: e.tensor_tensor(tv("bm"), tv("ba"), tv("Mr"), OP.mult))
            p.emit("vector", lambda e: e.tensor_tensor(tv("am2"), tv("bm"), tv("bm"), OP.mult))
            p.emit("vector", lambda e, i=i: e.tensor_scalar(
                am2c[:, i:i + 1], tv("am2"), EPS, None, OP.max))
            p.emit("vector", lambda e: e.tensor_tensor(tv("rb2"), tv("ba"), tv("ba"), OP.mult))
            p.emit("vector", lambda e, i=i: e.reciprocal(tv("rc"), am2c[:, i:i + 1]))
            p.emit("vector", lambda e: e.tensor_tensor(tv("g0"), tv("rb2"), tv("rc"), OP.mult))
            gv = p.emit("vector", lambda e: e.tensor_scalar(tv("g127"), tv("g0"), 127.0, None, OP.mult),
                        inc="V")
            st["sg"] = p.emit("scalar", lambda e: e.sqrt(tc("sqg"), tv("g127")), waits=[gv], inc="C")
            st["pk20"] = st_state.get(i - 1, {}).get("pk20")
            st["pk21"] = st_state.get(i - 1, {}).get("pk21")

        def emit_epi_chunk(i, o):
            st = st_state[i]
            s_ = o % 2
            bsl = slice(512 * s_, 512 * (s_ + 1))
            eh = e8[:, bsl]; zh = z8[:, bsl]; zih = zi8[:, bsl]
            sh = s8[:, bsl]; mh = mk8[:, 512 * s_:512 * (s_ + 1)]
            sl = slice(512 * o, 512 * (o + 1))
            pk = st["pk2%d" % s_]
            ee = p.emit("scalar", lambda e, sl=sl, eh=eh: e.activation(
                eh, hb[:, sl], AF.Relu, scale=tc("sqg")), waits=[st["sg"]], inc="C")
            zz = p.emit("scalar", lambda e, eh=eh, zh=zh: e.square(zh, eh), waits=[ee, pk], inc="C")
            zc = p.emit("scalar", lambda e, zh=zh, zih=zih: e.activation(zih, zh, AF.Copy),
                        waits=[zz], inc="C")
            p.emit("vector", lambda e, zh=zh, zih=zih, sh=sh: e.tensor_tensor(
                sh, zh, zih, OP.subtract),
                waits=[zc] + ([k2b_free[0]] if o <= 1 else []))
            p.emit("vector", lambda e, sh=sh, zih=zih, zh=zh: e.scalar_tensor_tensor(
                zh, sh, -0.5, zih, OP.is_lt, OP.subtract))
            tok = p.emit("vector", lambda e, sl=sl, sh=sh, zh=zh: e.scalar_tensor_tensor(
                k2b[:, sl], sh, 0.5, zh, OP.is_gt, OP.subtract), waits=bar, inc="V")
            st["pk2%d" % s_] = tok
            st.setdefault("k2toks", {})[o // 2 if False else o] = tok
            return tok

        def emit_mm_quarter(i, q):
            st = st_state[i]
            pu = psU[q % 2]
            uws = psU_users[q % 2]
            psU_users[q % 2] = []
            mm_last = None
            for c in range(16):
                for j in range(2):
                    mm_last = p.emit("tensor", lambda e, c=c, j=j, q=q, pu=pu, kT=st["kT"]: e.matmul(
                        pu[:, 512 * j:512 * (j + 1)],
                        kT[:, 128 * c:128 * (c + 1)],
                        wT[:, 4096 * c + 1024 * q + 512 * j:4096 * c + 1024 * q + 512 * (j + 1)],
                        start=(c == 0), stop=(c == 15)),
                        waits=(st["ktr"] + uws + [ternV_last] + bar) if c == 0 else (), inc="T")
            hb_guard = (st_state.get(i - 1, {}).get("mr") or [None] * 4)[q]
            ev = p.emit("scalar", lambda e, q=q, pu=pu: e.activation(
                hb[:, 1024 * q:1024 * (q + 1)], pu[:], AF.Copy),
                waits=[mm_last, hb_guard], inc="C")
            psU_users[q % 2].append(ev)
            st.setdefault("evs", []).append(ev)

        for i in range(NT + 1):
            if i < NT:
                emit_front(i)
            if i >= 1:
                emit_gamma(i - 1)
            if i < NT:
                emit_front_b(i)
            for q in range(4):
                if i >= 1:
                    emit_epi_chunk(i - 1, 2 * q)
                    emit_epi_chunk(i - 1, 2 * q + 1)
                if i < NT:
                    emit_mm_quarter(i, q)
            if i < NT:
                st = st_state[i]
                mr = p.emit("vector", lambda e: e.tensor_reduce(
                    tv("Mq0"), hb[:], AX, OP.max), waits=st["evs"], inc="V")
                st["mr"] = [mr, mr, mr, mr]
            if i >= 1:
                k2b_free[0] = p.emit("scalar", lambda e, j=i - 1: e.dma_start(
                    k2_hbm[128 * j:128 * (j + 1), :], k2b),
                    waits=[st_state[i - 1]["k2toks"][6], st_state[i - 1]["k2toks"][7]],
                    inc="SK", amt=16)
                st_state.pop(i - 2, None)
            emit_wdn(4)
        emit_wdn(99)

        bar2 = p.barrier()

        td_toks = list(bar2)
        for f in range(32):
            td_toks.append(p.emit("sync", lambda e, f=f: e.dma_start(
                wT[:, 2048 * f:2048 * (f + 1)], td_hbm[128 * f:128 * (f + 1), :]),
                waits=bar2, inc="LT", amt=16))
        dtok = p.emit("vector", lambda e: e.tensor_scalar(dall[:], am2c[:], sdn_rep2, 1.0 / 127.0,
                                                          OP.mult, OP.mult), waits=bar2, inc="V")

        xbd_ring = Ring([f32(0, D), f32(2048, D)], ["L0", "L1"])
        x1d = f32(4096, D)
        yb = f32(6144, D)
        k2r_ring = Ring([pf16[:, 0:4096], pf16[:, 4096:8192]], ["K0", "K1"])
        k2T_one = pf16[:, 8192:12288]
        psD = [PA[:, 0:1024], PA[:, 1024:2048]]
        psD_users = [list(bar2), list(bar2)]
        psT_users2 = [list(bar2), list(bar2)]
        yb_users = []
        y_last = None
        for i in range(NT):
            slot, ws = xbd_ring.acquire(i)
            ld = p.emit("sync", lambda e, d=slot, i=i: e.dma_start(d[:], x_own[128 * i:128 * (i + 1), :]),
                        waits=ws + bar2, inc=xbd_ring.sem(i), amt=16)
            k2s, k2w = k2r_ring.acquire(i)
            l2 = p.emit("sync", lambda e, d=k2s, i=i: e.dma_start(d[:], k2_hbm[128 * i:128 * (i + 1), :]),
                        waits=k2w + bar2, inc=k2r_ring.sem(i), amt=16)
            k2T, kws = k2T_one, []
            ktr_evs = []
            for g4 in range(8):
                pt = psT[g4 % 2]
                tws = psT_users2[g4 % 2]
                psT_users2[g4 % 2] = []
                mts = []
                for c in range(4):
                    cc = g4 * 4 + c
                    mts.append(p.emit("tensor", lambda e, cc=cc, c=c, pt=pt, k2s=k2s: e.matmul(
                        pt[:, 128 * c:128 * (c + 1)], k2s[:, 128 * cc:128 * (cc + 1)], id16[:],
                        start=True, stop=True),
                        waits=[l2] + tws + kws + (td_toks if i == 0 and g4 == 0 else []), inc="T"))
                ev = p.emit("scalar", lambda e, g4=g4, pt=pt, k2T=k2T: e.activation(
                    k2T[:, 512 * g4:512 * (g4 + 1)], pt[:], AF.Copy), waits=mts, inc="C")
                psT_users2[g4 % 2].append(ev)
                ktr_evs.append(ev)
            k2r_ring.used(i, ktr_evs[-1])
            x1t = p.emit("vector", lambda e, s=slot: e.tensor_tensor(x1d, s, c_rep[:], OP.add),
                         waits=[ld, y_last], inc="V")
            xbd_ring.used(i, x1t)
            for h in range(2):
                pd = psD[h % 2]
                dws = psD_users[h % 2]
                psD_users[h % 2] = []
                mm_last = None
                for c in range(32):
                    for j in range(2):
                        mm_last = p.emit("tensor", lambda e, c=c, j=j, h=h, pd=pd, k2T=k2T: e.matmul(
                            pd[:, 512 * j:512 * (j + 1)],
                            k2T[:, 128 * c:128 * (c + 1)],
                            wT[:, 2048 * c + 1024 * h + 512 * j:2048 * c + 1024 * h + 512 * (j + 1)],
                            start=(c == 0), stop=(c == 31)),
                            waits=(ktr_evs + dws + td_toks) if c == 0 else (), inc="T")
                ov = p.emit("vector", lambda e, h=h, pd=pd, i=i: e.scalar_tensor_tensor(
                    yb[:, 1024 * h:1024 * (h + 1)], pd[:], dall[:, i:i + 1],
                    x1d[:, 1024 * h:1024 * (h + 1)], OP.mult, OP.add),
                    waits=[mm_last, x1t, dtok] + yb_users, inc="V")
                psD_users[h % 2].append(ov)
                yb_users = []
                if h == 1:
                    y_last = p.emit("scalar", lambda e, i=i: e.dma_start(
                        y_out[128 * i:128 * (i + 1), :], yb), waits=[ov], inc="SY", amt=16)
                    yb_users = [y_last]

        final = ("SY", p.cnt["SY"])
        p.emit("scalar", lambda e: e.nop(), waits=[final])
        p.emit("sync", lambda e: e.nop(), waits=[final])

        emit = p.emitters(sems)
        block.sync(emit["sync"])
        block.scalar(emit["scalar"])
        block.vector(emit["vector"])
        block.tensor(emit["tensor"])
        block.gpsimd(emit["gpsimd"])

    return nc


_CACHE = {}


def kernel(L0_attn_x, L0_attn_Wq, L0_attn_bq, L0_attn_Wk, L0_attn_bk,
           L0_attn_Wv, L0_attn_bv, L0_attn_Wo, L0_attn_bo,
           L0_ffn_w_up, L0_ffn_w_down):
    x = np.ascontiguousarray(np.asarray(L0_attn_x, dtype=np.float32)).reshape(16384, D)
    Wv_ = np.ascontiguousarray(np.asarray(L0_attn_Wv, dtype=np.float32))
    Wo_ = np.ascontiguousarray(np.asarray(L0_attn_Wo, dtype=np.float32))
    bv = np.asarray(L0_attn_bv, dtype=np.float32)
    bo = np.asarray(L0_attn_bo, dtype=np.float32)
    wuT = np.ascontiguousarray(np.asarray(L0_ffn_w_up, dtype=np.float32).T)
    wdT = np.ascontiguousarray(np.asarray(L0_ffn_w_down, dtype=np.float32).T)
    bvT = np.ascontiguousarray(bv.reshape(16, 128).T)
    boT = np.ascontiguousarray(bo.reshape(16, 128).T)
    id16 = np.eye(128, dtype=np.float16)
    id32 = np.eye(128, dtype=np.float32)
    ones1 = np.ones((1, 128), dtype=np.float32)
    onec = np.ones((128, 1), dtype=np.float32)

    if "nc" not in _CACHE:
        _CACHE["nc"] = build_nc()
    nc = _CACHE["nc"]

    in_maps = []
    for c in range(8):
        b = c // 2
        half = c % 2
        own = x[b * 4096 + half * 2048: b * 4096 + half * 2048 + 2048]
        sib = x[b * 4096 + (1 - half) * 2048: b * 4096 + (1 - half) * 2048 + 2048]
        in_maps.append({
            "x_own": own, "x_sib": sib, "Wv": Wv_, "Wo": Wo_,
            "wuT": wuT, "wdT": wdT, "bvT": bvT, "boT": boT,
            "id16": id16, "id32": id32, "ones1": ones1, "onec": onec,
        })
    res = run_bass_kernel_spmd(nc, in_maps, core_ids=list(range(8)))
    y = np.empty((16384, D), dtype=np.float32)
    for c in range(8):
        b = c // 2
        half = c % 2
        y[b * 4096 + half * 2048: b * 4096 + half * 2048 + 2048] = res.results[c]["y"]
    return y.reshape(4, 4096, D)


# version 23
# speedup vs baseline: 1.0579x; 1.0486x over previous
import numpy as np
import concourse.bass as bass
import concourse.mybir as mybir
from concourse.bass_utils import run_bass_kernel_spmd

F32 = mybir.dt.float32
F16 = mybir.dt.float16
F8 = mybir.dt.float8e4
I32 = mybir.dt.int32
OP = mybir.AluOpType
AF = mybir.ActivationFunctionType
AX = mybir.AxisListType.X

D = 2048
FF = 4096
TOK = 2048
NT = 16
EPS = 1e-5

ENGS = ("sync", "scalar", "vector", "tensor", "gpsimd")
DEBUG = False


class Prog:
    def __init__(self):
        self.ops = {e: [] for e in ENGS}
        self.cnt = {}

    def ch(self, name):
        self.cnt[name] = 0

    ENG_CH = {"vector": "V", "scalar": "C", "gpsimd": "G"}

    def emit(self, eng, fn, waits=(), inc=None, amt=1):
        if inc is None and eng in self.ENG_CH:
            inc = self.ENG_CH[eng]
            amt = 1
        tok = None
        if inc is not None:
            self.cnt[inc] += amt
            tok = (inc, self.cnt[inc])
        ws = tuple(w for w in waits if w is not None)
        self.ops[eng].append((fn, ws, inc, amt))
        return tok

    def barrier(self):
        return [(c, n) for c, n in self.cnt.items()]

    def emitters(self, sems):
        last = {}
        COMPUTE = ("V", "C", "G")

        def mk(name):
            def run(eng):
                pending = None
                for fn, waits, inc, amt in self.ops[name]:
                    if pending is not None:
                        s, v = pending
                        k = (name, s)
                        if last.get(k, 0) < v:
                            eng.wait_ge(sems[s], v)
                            last[k] = v
                        pending = None
                    for (s, v) in waits:
                        k = (name, s)
                        if last.get(k, 0) < v:
                            eng.wait_ge(sems[s], v)
                            last[k] = v
                    inst = fn(eng)
                    if inc is not None:
                        inst.then_inc(sems[inc], amt)
                        if inc in COMPUTE:
                            self_tok = None
                            pending = (inc, self._emitted_cnt[name][inc])
                for _ in ():
                    pass
            return run
        self._emitted_cnt = {e: {} for e in ENGS}
        for e in ENGS:
            cnt = {}
            vals = []
            for fn, waits, inc, amt in self.ops[e]:
                if inc is not None:
                    cnt[inc] = cnt.get(inc, 0) + amt
                    vals.append(cnt[inc])
                else:
                    vals.append(None)
            self._emitted_cnt[e] = (cnt, vals)

        def mk2(name):
            _, vals = self._emitted_cnt[name]

            def run(eng):
                pending = None
                for idx, (fn, waits, inc, amt) in enumerate(self.ops[name]):
                    if pending is not None:
                        s, v = pending
                        k = (name, s)
                        if last.get(k, 0) < v:
                            eng.wait_ge(sems[s], v)
                            last[k] = v
                        pending = None
                    for (s, v) in waits:
                        k = (name, s)
                        if last.get(k, 0) < v:
                            eng.wait_ge(sems[s], v)
                            last[k] = v
                    inst = fn(eng)
                    if inc is not None:
                        inst.then_inc(sems[inc], amt)
                        if inc in COMPUTE:
                            pending = (inc, vals[idx])
            return run
        return {e: mk2(e) for e in ENGS}


class Ring:
    def __init__(self, slots, semnames=None):
        self.slots = list(slots)
        self.n = len(self.slots)
        self.users = [[] for _ in self.slots]
        self.sems = semnames or [None] * self.n

    def acquire(self, i):
        s = i % self.n
        waits = self.users[s]
        self.users[s] = []
        return self.slots[s], waits

    def sem(self, i):
        return self.sems[i % self.n]

    def used(self, i, tok):
        if tok is not None:
            self.users[i % self.n].append(tok)


def build_nc():
    nc = bass.Bass("TRN2", target_bir_lowering=False, detect_race_conditions=False)

    x_own = nc.dram_tensor("x_own", [TOK, D], F32, kind="ExternalInput").ap()
    x_sib = nc.dram_tensor("x_sib", [TOK, D], F32, kind="ExternalInput").ap()
    Wv = nc.dram_tensor("Wv", [D, D], F32, kind="ExternalInput").ap()
    Wo = nc.dram_tensor("Wo", [D, D], F32, kind="ExternalInput").ap()
    wuT = nc.dram_tensor("wuT", [D, FF], F32, kind="ExternalInput").ap()
    wdT = nc.dram_tensor("wdT", [FF, D], F32, kind="ExternalInput").ap()
    bvT_h = nc.dram_tensor("bvT", [128, 16], F32, kind="ExternalInput").ap()
    boT_h = nc.dram_tensor("boT", [128, 16], F32, kind="ExternalInput").ap()
    id16_h = nc.dram_tensor("id16", [128, 128], F16, kind="ExternalInput").ap()
    id32_h = nc.dram_tensor("id32", [128, 128], F32, kind="ExternalInput").ap()
    ones1_h = nc.dram_tensor("ones1", [1, 128], F32, kind="ExternalInput").ap()
    onec_h = nc.dram_tensor("onec", [128, 1], F32, kind="ExternalInput").ap()
    y_out = nc.dram_tensor("y", [TOK, D], F32, kind="ExternalOutput").ap()
    k2_hbm = nc.dram_tensor("k2s", [TOK, FF], F16, kind="Internal").ap()
    td_hbm = nc.dram_tensor("tds", [FF, D], F8, kind="Internal").ap()
    us_hbm = nc.dram_tensor("us", [16, 128], F32, kind="Internal").ap()
    if DEBUG:
        dbg_crep = nc.dram_tensor("dbg_crep", [128, D], F32, kind="ExternalOutput").ap()
        dbg_srep = nc.dram_tensor("dbg_srep", [128, 8], F32, kind="ExternalOutput").ap()
        dbg_kf = nc.dram_tensor("dbg_kf", [128, D], F16, kind="ExternalOutput").ap()
        dbg_hb = nc.dram_tensor("dbg_hb", [128, FF], F32, kind="ExternalOutput").ap()
        dbg_am2 = nc.dram_tensor("dbg_am2", [128, 16], F32, kind="ExternalOutput").ap()
        dbg_wt = nc.dram_tensor("dbg_wt", [128, 4096], F8, kind="ExternalOutput").ap()
        dbg_x1 = nc.dram_tensor("dbg_x1", [128, D], F32, kind="ExternalOutput").ap()
        dbg_tv = nc.dram_tensor("dbg_tv", [128, 64], F32, kind="ExternalOutput").ap()
        dbg_xrow = nc.dram_tensor("dbg_xrow", [1, D], F32, kind="ExternalOutput").ap()
        dbg_rep = nc.dram_tensor("dbg_rep", [128, D], F32, kind="ExternalOutput").ap()
        dbg_ucol = nc.dram_tensor("dbg_ucol", [128, 16], F32, kind="ExternalOutput").ap()
        dbg_uT = nc.dram_tensor("dbg_uT", [16, 128], F32, kind="ExternalOutput").ap()
        dbg_m1 = nc.dram_tensor("dbg_m1", [128, D], F16, kind="ExternalOutput").ap()
    cs_hbm = nc.dram_tensor("cs", [16, 128], F32, kind="Internal").ap()

    p = Prog()
    DMACH = ("LC", "L0", "L1", "W0", "W1", "LB", "LT", "K0", "K1",
             "U0", "U1", "U2", "U3", "L2", "L3", "D0", "D1", "SB", "SY", "SK", "SD0", "SD1", "SG")
    for c in ("T", "V", "C", "G") + DMACH:
        p.ch(c)

    from contextlib import ExitStack
    es = ExitStack()
    with es:
        id16 = es.enter_context(nc.sbuf_tensor("sb_id16", [128, 128], F16))
        id32 = es.enter_context(nc.sbuf_tensor("sb_id32", [128, 128], F32))
        ones1 = es.enter_context(nc.sbuf_tensor("sb_ones1", [1, 128], F32))
        onec = es.enter_context(nc.sbuf_tensor("sb_onec", [128, 1], F32))
        bvT = es.enter_context(nc.sbuf_tensor("sb_bvT", [128, 16], F32))
        boT = es.enter_context(nc.sbuf_tensor("sb_boT", [128, 16], F32))
        c_rep = es.enter_context(nc.sbuf_tensor("c_rep", [128, D], F32))
        am2c = es.enter_context(nc.sbuf_tensor("am2c", [128, 16], F32))
        wabs_u = es.enter_context(nc.sbuf_tensor("wabs_u", [128, 32], F32))
        wabs_d = es.enter_context(nc.sbuf_tensor("wabs_d", [128, 32], F32))
        ucol = es.enter_context(nc.sbuf_tensor("ucol", [128, 16], F32))
        ccol = es.enter_context(nc.sbuf_tensor("ccol", [128, 16], F32))
        uTsb = es.enter_context(nc.sbuf_tensor("uT", [16, 128], F32))
        cTsb = es.enter_context(nc.sbuf_tensor("cT", [16, 128], F32))
        tiny = es.enter_context(nc.sbuf_tensor("tiny", [128, 64], F32))
        tinyC = es.enter_context(nc.sbuf_tensor("tinyC", [128, 16], F32))
        xsall = es.enter_context(nc.sbuf_tensor("xsall", [128, 32], F32))
        ssqall = es.enter_context(nc.sbuf_tensor("ssqall", [128, 32], F32))
        aall = es.enter_context(nc.sbuf_tensor("aall", [128, 32], F32))
        maall = es.enter_context(nc.sbuf_tensor("maall", [128, 32], F32))
        p1s = es.enter_context(nc.sbuf_tensor("p1s", [128, 128], F32))
        srep = es.enter_context(nc.sbuf_tensor("srep", [128, 8], F32))
        dall = es.enter_context(nc.sbuf_tensor("dall", [128, 16], F32))
        t8a = es.enter_context(nc.sbuf_tensor("t8a", [128, 2048], F8))
        mkq = es.enter_context(nc.sbuf_tensor("mkq8", [128, 1024], F8))
        mk8 = es.enter_context(nc.sbuf_tensor("mk88", [128, 1024], F8))
        t8b = es.enter_context(nc.sbuf_tensor("t8b", [128, 2048], F8))
        pf32 = es.enter_context(nc.sbuf_tensor("pf32", [128, 23680], F32))
        pf16 = es.enter_context(nc.sbuf_tensor("pf16", [128, 12288], F16))
        pi32 = es.enter_context(nc.sbuf_tensor("pi32", [128, 2048], I32))
        wT = es.enter_context(nc.sbuf_tensor("wT", [128, 65536], F8))
        PA = es.enter_context(nc.psum_tensor("PA", [128, 2048], F32))
        PM = es.enter_context(nc.psum_tensor("PM", [128, 512], F32))
        PT2 = es.enter_context(nc.psum_tensor("PT2", [128, 512], F32))
        PS = es.enter_context(nc.psum_tensor("PS", [128, 512], F32))
        sems = {}
        for c in ("T", "V", "C", "G") + DMACH:
            sems[c] = es.enter_context(nc.semaphore("sem" + c))
        block = es.enter_context(nc.Block())
        def f32(off, n):
            return pf32[:, off:off + n]

        TC = {}

        def tc(name):
            if name not in TC:
                TC[name] = len(TC)
                assert len(TC) <= 16
            i = TC[name]
            return tinyC[:, i:i + 1]

        TV = {}

        def tv(name):
            if name not in TV:
                TV[name] = len(TV)
                assert len(TV) <= 56
            i = TV[name]
            return tiny[:, i:i + 1]

        onesm = f32(23552, 128)
        om = p.emit("vector", lambda e: e.memset(onesm, 1.0), inc="V")
        for dst, src in ((id16, id16_h), (id32, id32_h), (ones1, ones1_h),
                         (onec, onec_h), (bvT, bvT_h), (boT, boT_h)):
            p.emit("sync", lambda e, d=dst, s=src: e.dma_start(d[:], s[:]), inc="LC", amt=16)
        tok_const = ("LC", p.cnt["LC"])

        xb_ring = Ring([f32(0, D), f32(2048, D), f32(4096, D)], ["L0", "L1", "L2"])
        wu_ring = Ring([f32(6144, 4096), f32(10240, 4096)], ["U0", "U1"])
        wdnB_ring = Ring([f32(14336, 4096), f32(18432, 4096)], ["D0", "D1"])
        cdump = f32(12288, D)
        vdump = f32(14336, D)

        last_mm = None
        last_bs = None

        wdT2 = wdT.rearrange("(a b) c -> a (b c)", b=2)
        td2 = td_hbm.rearrange("(a b) c -> a (b c)", b=2)
        bnb = p1s[:, 64:88]
        wu_i = [0]
        wd_i = [0]
        absu_last = [None]
        absd_last2 = [None]

        def emit_absu(r):
            i = wu_i[0]; wu_i[0] += 1
            slot, ws = wu_ring.acquire(i)
            ld = p.emit("scalar", lambda e, d=slot, r=r: e.dma_start(
                d[:], wuT[128 * r:128 * (r + 1), :]), waits=ws, inc=wu_ring.sem(i), amt=16)
            p.emit("scalar", lambda e, s=slot, r=r: e.activation(
                s[:, 0:2048], s[:, 0:2048], AF.Abs, accum_out=wabs_u[:, 2 * r:2 * r + 1]),
                waits=[ld], inc="C")
            a2 = p.emit("scalar", lambda e, s=slot, r=r: e.activation(
                s[:, 2048:4096], s[:, 2048:4096], AF.Abs, accum_out=wabs_u[:, 2 * r + 1:2 * r + 2]),
                inc="C")
            wu_ring.used(i, a2)
            absu_last[0] = a2

        def emit_absd(r):
            i = wd_i[0]; wd_i[0] += 1
            slot, ws = wdnB_ring.acquire(i)
            ld = p.emit("gpsimd", lambda e, d=slot, r=r: e.dma_start(
                d[:], wdT2[128 * r:128 * (r + 1), :]), waits=ws, inc=wdnB_ring.sem(i), amt=16)
            p.emit("scalar", lambda e, s=slot, r=r: e.activation(
                s[:, 0:2048], s[:, 0:2048], AF.Abs, accum_out=wabs_d[:, 2 * r:2 * r + 1]),
                waits=[ld], inc="C")
            a2 = p.emit("scalar", lambda e, s=slot, r=r: e.activation(
                s[:, 2048:4096], s[:, 2048:4096], AF.Abs, accum_out=wabs_d[:, 2 * r + 1:2 * r + 2]),
                inc="C")
            wdnB_ring.used(i, a2)
            absd_last2[0] = a2

        ma_last = None
        for t in range(32):
            src_t = x_own if t < 16 else x_sib
            r0 = (t % 16) * 128
            slot, ws = xb_ring.acquire(t)
            ld = p.emit("sync", lambda e, d=slot, s=src_t, r=r0: e.dma_start(d[:], s[r:r + 128, :]),
                        waits=ws + [tok_const], inc=xb_ring.sem(t), amt=16)
            for q in range(4):
                p.emit("vector", lambda e, xbv=slot, q=q: e.bn_stats(
                    bnb[:, 6 * q:6 * (q + 1)], xbv[:, 512 * q:512 * (q + 1)]),
                    waits=[ld], inc="V")
            mvt = p1s[:, 2 * t:2 * t + 2]
            ag = p.emit("vector", lambda e, mvt=mvt: e.bn_aggr(mvt, bnb), inc="V")
            p.emit("vector", lambda e, mvt=mvt: e.tensor_scalar(
                tv("ve"), mvt[:, 1:2], 1.0, EPS, OP.mult, OP.add))
            rv = p.emit("vector", lambda e: e.reciprocal(tv("rq"), tv("ve")), inc="V")
            rc = p.emit("scalar", lambda e, t=t: e.sqrt(aall[:, t:t + 1], tv("rq")),
                        waits=[rv], inc="C")
            ma_last = p.emit("vector", lambda e, mvt=mvt, t=t: e.tensor_tensor(
                maall[:, t:t + 1], mvt[:, 0:1], aall[:, t:t + 1], OP.mult),
                waits=[rc], inc="V")
            for q in range(4):
                last_mm = p.emit("tensor", lambda e, xbv=slot, q=q, t=t,
                                 s0=(t == 0), s1=(t == 31): e.matmul(
                    PA[0:1, 512 * q:512 * (q + 1)], aall[:, t:t + 1],
                    xbv[:, 512 * q:512 * (q + 1)], start=s0, stop=s1),
                    waits=[rc, ld], inc="T")
                xb_ring.used(t, last_mm)
            xb_ring.used(t, ag)
            if t % 2 == 0:
                emit_absu(t // 2)
            else:
                emit_absd(t // 2)

        svd = p.emit("vector", lambda e: e.tensor_reduce(tv("wsd"), wabs_d[:], AX, OP.add),
                     waits=[absd_last2[0]], inc="V")
        tkd = p.emit("tensor", lambda e: e.matmul(PS[:, 17:18], onesm, tv("wsd"),
                                                  start=True, stop=True), waits=[svd, om], inc="T")
        rrd = p.emit("vector", lambda e: e.tensor_copy(tv("draw"), PS[:, 17:18]),
                     waits=[tkd], inc="V")
        p.emit("vector", lambda e: e.tensor_scalar(srep[:, 2:3], tv("draw"),
                                                   1.0 / (D * FF), EPS, OP.mult, OP.max))
        sdn_tok = p.emit("vector", lambda e: e.reciprocal(srep[:, 3:4], srep[:, 2:3]), inc="V")
        thd_tok = p.emit("vector", lambda e: e.tensor_scalar(
            srep[:, 5:6], srep[:, 2:3], 0.5, None, OP.mult), inc="V")
        sdn_rep2 = srep[:, 2:3]
        ivd_rep2 = srep[:, 3:4]
        thd_rep = srep[:, 5:6]

        sv = p.emit("vector", lambda e: e.tensor_reduce(tv("wsu"), wabs_u[:], AX, OP.add),
                    waits=[absu_last[0]], inc="V")
        tksu = p.emit("tensor", lambda e: e.matmul(PS[:, 32:33], onesm, tv("wsu"),
                                                   start=True, stop=True),
                      waits=[sv, om], inc="T")
        rr = p.emit("vector", lambda e: e.tensor_copy(tv("wraw"), PS[:, 32:33]),
                    waits=[tksu], inc="V")
        p.emit("vector", lambda e: e.tensor_scalar(srep[:, 0:1], tv("wraw"),
                                                   1.0 / (D * FF), EPS, OP.mult, OP.max))
        bu = p.emit("vector", lambda e: e.reciprocal(srep[:, 1:2], srep[:, 0:1]), inc="V")
        thu_tok = p.emit("vector", lambda e: e.tensor_scalar(
            srep[:, 4:5], srep[:, 0:1], 0.5, None, OP.mult), inc="V")
        sup_rep = srep[:, 0:1]
        ivu_rep = srep[:, 1:2]
        thu_rep = srep[:, 4:5]

        last_bs = p.emit("tensor", lambda e: e.matmul(PS[:, 0:32], onesm, maall[:],
                                                      start=True, stop=True),
                         waits=[rr, om], inc="T")

        m1b = pf16[:, 0:4096]
        m2b = pf16[:, 4096:8192]
        t8_rP = Ring([t8a, t8b], ["SD0", "SD1"])
        ternV_last = None
        ternd_P_toks = []
        for r in range(16):
            i = wu_i[0]; wu_i[0] += 1
            slot, ws = wu_ring.acquire(i)
            ld = p.emit("scalar", lambda e, d=slot, r=r: e.dma_start(
                d[:], wuT[128 * r:128 * (r + 1), :]), waits=ws + [thu_tok], inc=wu_ring.sem(i), amt=16)
            p.emit("vector", lambda e, s=slot: e.tensor_scalar(
                m1b, s, -1.0, thu_rep, OP.mult, OP.is_gt), waits=[ld, thu_tok])
            ternV_last = p.emit("vector", lambda e, s=slot, r=r: e.scalar_tensor_tensor(
                wT[:, 4096 * r:4096 * (r + 1)], s, thu_rep, m1b, OP.is_gt, OP.subtract), inc="V")
            wu_ring.used(i, ternV_last)

            i = wd_i[0]; wd_i[0] += 1
            slot, ws = wdnB_ring.acquire(i)
            ld = p.emit("gpsimd", lambda e, d=slot, r=r: e.dma_start(
                d[:], wdT2[128 * r:128 * (r + 1), :]), waits=ws + [thd_tok],
                inc=wdnB_ring.sem(i), amt=16)
            p.emit("vector", lambda e, s=slot: e.tensor_scalar(
                m1b, s, -1.0, thd_rep, OP.mult, OP.is_gt), waits=[ld, thd_tok])
            ta, taw = t8_rP.acquire(2 * r)
            t1 = p.emit("vector", lambda e, s=slot, o=ta: e.scalar_tensor_tensor(
                o[:], s[:, 0:2048], thd_rep, m1b[:, 0:2048], OP.is_gt, OP.subtract),
                waits=taw, inc="V")
            sa = p.emit("sync", lambda e, r=r, o=ta: e.dma_start(
                td2[128 * r:128 * (r + 1), 0:2048], o[:]), waits=[t1],
                inc=t8_rP.sem(2 * r), amt=16)
            t8_rP.used(2 * r, sa)
            tb, tbw = t8_rP.acquire(2 * r + 1)
            t2 = p.emit("vector", lambda e, s=slot, o=tb: e.scalar_tensor_tensor(
                o[:], s[:, 2048:4096], thd_rep, m1b[:, 2048:4096], OP.is_gt, OP.subtract),
                waits=tbw, inc="V")
            sb = p.emit("sync", lambda e, r=r, o=tb: e.dma_start(
                td2[128 * r:128 * (r + 1), 2048:4096], o[:]), waits=[t2],
                inc=t8_rP.sem(2 * r + 1), amt=16)
            t8_rP.used(2 * r + 1, sb)
            wdnB_ring.used(i, t2)
            ternV_last = t2
            ternd_P_toks.append(sb)

        xrow = pf32[0:1, 18432:18432 + D]
        e1 = p.emit("vector", lambda e: e.tensor_copy(xrow, PA[0:1, :]), waits=[last_mm], inc="V")
        bs = p.emit("vector", lambda e: e.tensor_reduce(tv("bsr"), PS[:, 0:32], AX, OP.add),
                    waits=[last_bs], inc="V")
        rep = f32(16384, D)
        brd = []
        for q in range(4):
            brd.append(p.emit("tensor", lambda e, q=q: e.matmul(
                PA[:, 512 * q:512 * (q + 1)], ones1[:], xrow[0:1, 512 * q:512 * (q + 1)],
                start=True, stop=True), waits=[e1], inc="T"))
        re1 = p.emit("vector", lambda e: e.tensor_scalar(rep, PA[:], tv("bsr"), 1.0 / 4096.0,
                                                         OP.subtract, OP.mult),
                     waits=brd + [bs], inc="V")
        if DEBUG:
            p.emit("scalar", lambda e: e.dma_start(dbg_xrow[:], rep[0:1, :]), waits=[re1], inc="SG", amt=16)

        if DEBUG:
            p.emit("scalar", lambda e: e.dma_start(dbg_rep[:], rep), waits=[re1], inc="SG", amt=16)
        wvo_ring = Ring([f32(8192, D), f32(10240, D)], ["W0", "W1"])
        ttr_last = None
        for t in range(16):
            slot, ws = wvo_ring.acquire(t)
            ld = p.emit("sync", lambda e, d=slot, t=t: e.dma_start(d[:], Wv[128 * t:128 * (t + 1), :]),
                        waits=ws, inc=wvo_ring.sem(t), amt=16)
            ttr_last = p.emit("vector", lambda e, s=slot, t=t: e.scalar_tensor_tensor(
                vdump, s, 1.0, rep, OP.mult, OP.mult, accum_out=ucol[:, t:t + 1]),
                waits=[ld, re1], inc="V")
            wvo_ring.used(t, ttr_last)
        ub = p.emit("vector", lambda e: e.tensor_tensor(ucol[:], ucol[:], bvT[:], OP.add),
                    waits=[ttr_last], inc="V")
        if DEBUG:
            p.emit("scalar", lambda e: e.dma_start(dbg_ucol[:], ucol[:]), waits=[ub], inc="SG", amt=16)
        tk = p.emit("tensor", lambda e: e.matmul(PM[0:16, 0:128], ucol[:], id32[:],
                                                 start=True, stop=True), waits=[ub], inc="T")
        ue = p.emit("vector", lambda e: e.tensor_copy(uTsb[:], PM[0:16, 0:128]), waits=[tk], inc="V")
        if DEBUG:
            p.emit("scalar", lambda e: e.dma_start(dbg_uT[:], uTsb[:]), waits=[ue], inc="SG", amt=16)
        ust = p.emit("scalar", lambda e: e.dma_start(us_hbm[:], uTsb[:]), waits=[ue], inc="SB", amt=16)
        uld = p.emit("sync", lambda e: e.dma_start(
            xrow, us_hbm.rearrange("a b -> (a b)")[None, :]), waits=[ust, e1] + brd, inc="LB", amt=16)
        brd2 = []
        for q in range(4):
            brd2.append(p.emit("tensor", lambda e, q=q: e.matmul(
                PA[:, 512 * q:512 * (q + 1)], ones1[:], xrow[0:1, 512 * q:512 * (q + 1)],
                start=True, stop=True), waits=[uld, re1], inc="T"))
        re2 = p.emit("scalar", lambda e: e.activation(rep, PA[:], AF.Copy),
                     waits=brd2 + [ttr_last], inc="C")

        tto_last = None
        for t in range(16):
            slot, ws = wvo_ring.acquire(16 + t)
            ld = p.emit("sync", lambda e, d=slot, t=t: e.dma_start(d[:], Wo[128 * t:128 * (t + 1), :]),
                        waits=ws, inc=wvo_ring.sem(16 + t), amt=16)
            tto_last = p.emit("vector", lambda e, s=slot, t=t: e.scalar_tensor_tensor(
                vdump, s, 1.0, rep, OP.mult, OP.mult, accum_out=ccol[:, t:t + 1]),
                waits=[ld, re2], inc="V")
            wvo_ring.used(16 + t, tto_last)
        cb = p.emit("vector", lambda e: e.tensor_tensor(ccol[:], ccol[:], boT[:], OP.add),
                    waits=[tto_last], inc="V")
        tk = p.emit("tensor", lambda e: e.matmul(PM[0:16, 0:128], ccol[:], id32[:],
                                                 start=True, stop=True), waits=[cb, ue], inc="T")
        ce = p.emit("vector", lambda e: e.tensor_copy(cTsb[:], PM[0:16, 0:128]), waits=[tk], inc="V")
        cst = p.emit("scalar", lambda e: e.dma_start(cs_hbm[:], cTsb[:]), waits=[ce], inc="SB", amt=16)
        cld = p.emit("sync", lambda e: e.dma_start(
            xrow, cs_hbm.rearrange("a b -> (a b)")[None, :]), waits=[cst, uld] + brd2, inc="LB", amt=16)
        brd3 = []
        for q in range(4):
            brd3.append(p.emit("tensor", lambda e, q=q: e.matmul(
                PA[:, 512 * q:512 * (q + 1)], ones1[:], xrow[0:1, 512 * q:512 * (q + 1)],
                start=True, stop=True), waits=[cld, re2], inc="T"))
        crep_tok = p.emit("scalar", lambda e: e.activation(c_rep[:], PA[:], AF.Copy),
                          waits=brd3, inc="C")

        bar = p.barrier()

        xbu_ring = Ring([f32(0, D), f32(2048, D)], ["L0", "L1"])
        x1b = f32(4096, D)
        hb = f32(8192, FF)
        zq = f32(18432, 1024)
        mq = f32(19456, 1024)
        e8 = f32(20480, 1024)
        z8 = f32(21504, 1024)
        s8 = f32(22528, 1024)
        wdn_ring = Ring([f32(14336, D), f32(16384, D)], ["D0", "D1"])
        kf = pf16[:, 0:2048]
        dm1 = pf16[:, 2048:4096]
        dm2 = pf16[:, 4096:6144]
        kT_one = pf16[:, 6144:8192]
        k2b = pf16[:, 8192:12288]
        ziq = pi32[:, 0:1024]
        zi8 = pi32[:, 1024:2048]

        psU = [PA[:, 0:1024], PA[:, 1024:2048]]
        psU_users = [[crep_tok], [crep_tok]]
        psT = [PM, PT2]
        psT_users = [[ce, bu], [ce, bu]]

        wdn_items = []
        wdn_iter = iter(wdn_items)
        wdn_i = [0]
        absd_last = [None]
        sdn_done = [None]
        ternd_toks = []
        t8_ring = Ring([t8a, t8b], ["SD0", "SD1"])
        sdn_rep = srep[:, 2:3]
        ivd_rep = srep[:, 3:4]
        dm_prev = [None]

        def emit_wdn(n):
            for _ in range(n):
                it = next(wdn_iter, None)
                if it is None:
                    return
                kind, r = it
                if kind == "sdn":
                    sv = p.emit("vector", lambda e: e.tensor_reduce(tv("wsd"), wabs_d[:], AX, OP.add),
                                waits=[absd_last2[0]] + bar, inc="V")
                    tk = p.emit("tensor", lambda e: e.matmul(PS[:, 17:18], onesm, tv("wsd"),
                                                             start=True, stop=True),
                                waits=[sv] + bar, inc="T")
                    rr = p.emit("vector", lambda e: e.tensor_copy(tv("draw"), PS[:, 17:18]),
                                waits=[tk], inc="V")
                    p.emit("vector", lambda e: e.tensor_scalar(srep[:, 2:3], tv("draw"),
                                                               1.0 / (D * FF), EPS, OP.mult, OP.max))
                    sdn_done[0] = p.emit("vector", lambda e: e.reciprocal(srep[:, 3:4], srep[:, 2:3]),
                                         inc="V")
                else:
                    i = wdn_i[0]; wdn_i[0] += 1
                    slot, ws = wdn_ring.acquire(i)
                    ld = p.emit("sync", lambda e, d=slot, r=r: e.dma_start(
                        d[:], wdT[128 * r:128 * (r + 1), :]), waits=ws + [sdn_done[0]],
                        inc=wdn_ring.sem(i), amt=16)
                    g1 = p.emit("vector", lambda e, s=slot: e.tensor_scalar(
                        dm1, s, ivd_rep, 0.5, OP.mult, OP.is_gt),
                        waits=[ld, sdn_done[0], dm_prev[0]])
                    g2 = p.emit("vector", lambda e, s=slot: e.tensor_scalar(
                        dm2, s, ivd_rep, -0.5, OP.mult, OP.is_lt), waits=[ld])
                    t8, t8w = t8_ring.acquire(len(ternd_toks))
                    tt = p.emit("vector", lambda e, o=t8: e.tensor_tensor(o[:], dm1, dm2, OP.subtract),
                                waits=[g2, g1] + t8w, inc="V")
                    dm_prev[0] = tt
                    st = p.emit("scalar", lambda e, r=r, o=t8: e.dma_start(
                        td_hbm[128 * r:128 * (r + 1), :], o[:]), waits=[tt],
                        inc=t8_ring.sem(len(ternd_toks)), amt=16)
                    t8_ring.used(len(ternd_toks), st)
                    wdn_ring.used(i, tt)
                    ternd_toks.append(st)

        prev_sq = [None]
        kf_free = [None]
        k2b_free = [None]
        st_state = {}

        front_nb = [None]

        def emit_front(i):
            slot, ws = xbu_ring.acquire(i)
            ld = p.emit("sync", lambda e, d=slot, i=i: e.dma_start(d[:], x_own[128 * i:128 * (i + 1), :]),
                        waits=ws + bar, inc=xbu_ring.sem(i), amt=16)
            x1 = p.emit("vector", lambda e, s=slot: e.scalar_tensor_tensor(
                x1b, s, 0.0, c_rep[:], OP.add, OP.add, accum_out=tv("xs")),
                waits=[ld, crep_tok, prev_sq[0]] + bar, inc="V")
            xbu_ring.used(i, x1)
            p.emit("vector", lambda e: e.tensor_reduce(tv("rmax"), x1b, AX, OP.max))
            p.emit("vector", lambda e: e.tensor_reduce(tv("rmin"), x1b, AX, OP.min))
            prev_sq[0] = p.emit("scalar", lambda e: e.activation(
                cdump, x1b, AF.Square, accum_out=tc("ssq")), waits=[x1] + bar, inc="C")
            p.emit("vector", lambda e: e.tensor_scalar(tv("mean"), tv("xs"), 1.0 / D, None, OP.mult))
            p.emit("vector", lambda e: e.tensor_tensor(tv("me2"), tv("mean"), tv("mean"), OP.mult))
            p.emit("vector", lambda e: e.tensor_scalar(tv("v1"), tc("ssq"), 1.0 / D, None, OP.mult),
                   waits=[prev_sq[0]])
            p.emit("vector", lambda e: e.tensor_tensor(tv("var"), tv("v1"), tv("me2"), OP.subtract))
            p.emit("vector", lambda e: e.tensor_scalar(tv("ve"), tv("var"), EPS, None, OP.add))
            rv = p.emit("vector", lambda e: e.reciprocal(tv("rq"), tv("ve")), inc="V")
            rs = p.emit("scalar", lambda e: e.sqrt(tc("rstd"), tv("rq")), waits=[rv], inc="C")
            p.emit("vector", lambda e: e.tensor_tensor(tv("t1"), tv("rmax"), tv("mean"), OP.subtract),
                   waits=[rs])
            p.emit("vector", lambda e: e.tensor_tensor(tv("t2"), tv("mean"), tv("rmin"), OP.subtract))
            p.emit("vector", lambda e: e.tensor_tensor(tv("t3"), tv("t1"), tv("t2"), OP.max))
            p.emit("vector", lambda e: e.tensor_tensor(tv("amax"), tv("t3"), tc("rstd"), OP.mult))
            amc = tv("amc%d" % (i % 2))
            front_amc = amc
            p.emit("vector", lambda e, amc=amc: e.tensor_scalar(amc, tv("amax"), EPS, None, OP.max))
            p.emit("vector", lambda e, amc=amc: e.tensor_scalar(tv("r127"), amc, 1.0 / 127.0, None, OP.mult))
            p.emit("vector", lambda e: e.reciprocal(tv("scl"), tv("r127")))
            p.emit("vector", lambda e: e.tensor_tensor(tv("sa"), tc("rstd"), tv("scl"), OP.mult))
            p.emit("vector", lambda e: e.tensor_tensor(tv("nb0"), tv("mean"), tv("sa"), OP.mult))
            nb = p.emit("vector", lambda e: e.tensor_scalar(tv("nbias"), tv("nb0"), -1.0, None, OP.mult),
                        inc="V")
            front_nb[0] = nb

        def emit_front_b(i):
            nb = front_nb[0]
            kq_last = None
            zcs = {}

            def q_zv(q):
                s_ = q % 2
                bsl = slice(512 * s_, 512 * (s_ + 1))
                zh = zq[:, bsl]
                sl = slice(512 * q, 512 * (q + 1))
                zv = p.emit("vector", lambda e, sl=sl, zh=zh: e.tensor_scalar(
                    zh, x1b[:, sl], tv("sa"), tv("nbias"), OP.mult, OP.add),
                    waits=[nb] + ([kf_free[0]] if q <= 1 else []), inc="V")
                zcs[q] = p.emit("scalar", lambda e, zh=zh, zih=ziq[:, bsl]: e.activation(
                    zih, zh, AF.Copy), waits=[zv] + bar, inc="C")

            def q_fix(q):
                s_ = q % 2
                bsl = slice(512 * s_, 512 * (s_ + 1))
                zh = zq[:, bsl]; mh = mq[:, bsl]; zih = ziq[:, bsl]
                sl = slice(512 * q, 512 * (q + 1))
                p.emit("vector", lambda e, zh=zh, zih=zih, mh2=mh: e.tensor_tensor(
                    mh2, zh, zih, OP.subtract), waits=[zcs[q]])
                p.emit("vector", lambda e, mh2=mh, zih=zih, zh=zh: e.scalar_tensor_tensor(
                    zh, mh2, -0.5, zih, OP.is_lt, OP.subtract))
                return p.emit("vector", lambda e, sl=sl, mh2=mh, zh=zh: e.scalar_tensor_tensor(
                    kf[:, sl], mh2, 0.5, zh, OP.is_gt, OP.subtract), inc="V")

            q_zv(0); q_zv(1)
            q_fix(0)
            q_zv(2)
            q_fix(1)
            q_zv(3)
            q_fix(2)
            kq_last = q_fix(3)
            kT, kws = kT_one, []
            ktr_evs = []
            tr_last = None
            for g4 in range(4):
                pt = psT[g4 % 2]
                tws = psT_users[g4 % 2]
                psT_users[g4 % 2] = []
                mts = []
                for c in range(4):
                    cc = g4 * 4 + c
                    tr_last = p.emit("tensor", lambda e, cc=cc, c=c, pt=pt: e.matmul(
                        pt[:, 128 * c:128 * (c + 1)], kf[:, 128 * cc:128 * (cc + 1)], id16[:],
                        start=True, stop=True),
                        waits=[kq_last] + tws + kws + bar, inc="T")
                    mts.append(tr_last)
                ev = p.emit("scalar", lambda e, g4=g4, pt=pt, kT=kT: e.activation(
                    kT[:, 512 * g4:512 * (g4 + 1)], pt[:], AF.Copy), waits=mts, inc="C")
                psT_users[g4 % 2].append(ev)
                ktr_evs.append(ev)
            kf_free[0] = tr_last
            st_state[i] = {"kT": kT, "ktr": ktr_evs, "amc": tv("amc%d" % (i % 2))}

        def emit_gamma(i):
            st = st_state[i]
            sg_prev = st_state.get(i - 1, {}).get("sg")
            p.emit("vector", lambda e: e.tensor_scalar(tv("Mr"), tv("Mq0"), 0.0, None, OP.max),
                   waits=st["mr"] + ([sg_prev] if sg_prev else []))
            p.emit("vector", lambda e, amc=st["amc"]: e.tensor_scalar(
                tv("ba"), amc, sup_rep, 1.0 / 127.0, OP.mult, OP.mult))
            p.emit("vector", lambda e: e.tensor_tensor(tv("bm"), tv("ba"), tv("Mr"), OP.mult))
            p.emit("vector", lambda e: e.tensor_tensor(tv("am2"), tv("bm"), tv("bm"), OP.mult))
            p.emit("vector", lambda e, i=i: e.tensor_scalar(
                am2c[:, i:i + 1], tv("am2"), EPS, None, OP.max))
            p.emit("vector", lambda e: e.tensor_tensor(tv("rb2"), tv("ba"), tv("ba"), OP.mult))
            p.emit("vector", lambda e, i=i: e.reciprocal(tv("rc"), am2c[:, i:i + 1]))
            p.emit("vector", lambda e: e.tensor_tensor(tv("g0"), tv("rb2"), tv("rc"), OP.mult))
            gv = p.emit("vector", lambda e: e.tensor_scalar(tv("g127"), tv("g0"), 127.0, None, OP.mult),
                        inc="V")
            st["sg"] = p.emit("scalar", lambda e: e.sqrt(tc("sqg"), tv("g127")), waits=[gv], inc="C")
            st["pk20"] = st_state.get(i - 1, {}).get("pk20")
            st["pk21"] = st_state.get(i - 1, {}).get("pk21")

        def emit_epi_chunk(i, o):
            st = st_state[i]
            s_ = o % 2
            bsl = slice(512 * s_, 512 * (s_ + 1))
            eh = e8[:, bsl]; zh = z8[:, bsl]; zih = zi8[:, bsl]
            sh = s8[:, bsl]; mh = mk8[:, 512 * s_:512 * (s_ + 1)]
            sl = slice(512 * o, 512 * (o + 1))
            pk = st["pk2%d" % s_]
            ee = p.emit("scalar", lambda e, sl=sl, eh=eh: e.activation(
                eh, hb[:, sl], AF.Relu, scale=tc("sqg")), waits=[st["sg"]], inc="C")
            zz = p.emit("scalar", lambda e, eh=eh, zh=zh: e.square(zh, eh), waits=[ee, pk], inc="C")
            zc = p.emit("scalar", lambda e, zh=zh, zih=zih: e.activation(zih, zh, AF.Copy),
                        waits=[zz], inc="C")
            p.emit("vector", lambda e, zh=zh, zih=zih, sh=sh: e.tensor_tensor(
                sh, zh, zih, OP.subtract),
                waits=[zc] + ([k2b_free[0]] if o <= 1 else []))
            p.emit("vector", lambda e, sh=sh, zih=zih, zh=zh: e.scalar_tensor_tensor(
                zh, sh, -0.5, zih, OP.is_lt, OP.subtract))
            tok = p.emit("vector", lambda e, sl=sl, sh=sh, zh=zh: e.scalar_tensor_tensor(
                k2b[:, sl], sh, 0.5, zh, OP.is_gt, OP.subtract), waits=bar, inc="V")
            st["pk2%d" % s_] = tok
            st.setdefault("k2toks", {})[o // 2 if False else o] = tok
            return tok

        def emit_mm_quarter(i, q):
            st = st_state[i]
            pu = psU[q % 2]
            uws = psU_users[q % 2]
            psU_users[q % 2] = []
            mm_last = None
            for c in range(16):
                for j in range(2):
                    mm_last = p.emit("tensor", lambda e, c=c, j=j, q=q, pu=pu, kT=st["kT"]: e.matmul(
                        pu[:, 512 * j:512 * (j + 1)],
                        kT[:, 128 * c:128 * (c + 1)],
                        wT[:, 4096 * c + 1024 * q + 512 * j:4096 * c + 1024 * q + 512 * (j + 1)],
                        start=(c == 0), stop=(c == 15)),
                        waits=(st["ktr"] + uws + [ternV_last] + bar) if c == 0 else (), inc="T")
            hb_guard = (st_state.get(i - 1, {}).get("mr") or [None] * 4)[q]
            ev = p.emit("scalar", lambda e, q=q, pu=pu: e.activation(
                hb[:, 1024 * q:1024 * (q + 1)], pu[:], AF.Copy),
                waits=[mm_last, hb_guard], inc="C")
            psU_users[q % 2].append(ev)
            st.setdefault("evs", []).append(ev)

        for i in range(NT + 1):
            if i < NT:
                emit_front(i)
                emit_front_b(i)
            if i >= 1:
                emit_gamma(i - 1)
            for q in range(4):
                if i >= 1:
                    emit_epi_chunk(i - 1, 2 * q)
                    emit_epi_chunk(i - 1, 2 * q + 1)
                if i < NT:
                    emit_mm_quarter(i, q)
            if i < NT:
                st = st_state[i]
                mr = p.emit("vector", lambda e: e.tensor_reduce(
                    tv("Mq0"), hb[:], AX, OP.max), waits=st["evs"], inc="V")
                st["mr"] = [mr, mr, mr, mr]
            if i >= 1:
                k2b_free[0] = p.emit("scalar", lambda e, j=i - 1: e.dma_start(
                    k2_hbm[128 * j:128 * (j + 1), :], k2b),
                    waits=[st_state[i - 1]["k2toks"][6], st_state[i - 1]["k2toks"][7]],
                    inc="SK", amt=16)
                st_state.pop(i - 2, None)
            emit_wdn(4)
        emit_wdn(99)

        bar2 = p.barrier()

        td_toks = list(bar2)
        for f in range(32):
            td_toks.append(p.emit("sync", lambda e, f=f: e.dma_start(
                wT[:, 2048 * f:2048 * (f + 1)], td_hbm[128 * f:128 * (f + 1), :]),
                waits=bar2, inc="LT", amt=16))
        dtok = p.emit("vector", lambda e: e.tensor_scalar(dall[:], am2c[:], sdn_rep2, 1.0 / 127.0,
                                                          OP.mult, OP.mult), waits=bar2, inc="V")

        xbd_ring = Ring([f32(0, D), f32(2048, D)], ["L0", "L1"])
        x1d = f32(4096, D)
        yb = f32(6144, D)
        k2r_ring = Ring([pf16[:, 0:4096], pf16[:, 4096:8192]], ["K0", "K1"])
        k2T_one = pf16[:, 8192:12288]
        psD = [PA[:, 0:1024], PA[:, 1024:2048]]
        psD_users = [list(bar2), list(bar2)]
        psT_users2 = [list(bar2), list(bar2)]
        yb_users = []
        y_last = None
        for i in range(NT):
            slot, ws = xbd_ring.acquire(i)
            ld = p.emit("sync", lambda e, d=slot, i=i: e.dma_start(d[:], x_own[128 * i:128 * (i + 1), :]),
                        waits=ws + bar2, inc=xbd_ring.sem(i), amt=16)
            k2s, k2w = k2r_ring.acquire(i)
            l2 = p.emit("sync", lambda e, d=k2s, i=i: e.dma_start(d[:], k2_hbm[128 * i:128 * (i + 1), :]),
                        waits=k2w + bar2, inc=k2r_ring.sem(i), amt=16)
            k2T = k2T_one
            ktr_evs = []
            dws = psD_users[0] + psD_users[1]
            psD_users[0] = []
            psD_users[1] = []
            mm_last_h = [None, None]

            def d_tgroup(g4):
                pt = psT[g4 % 2]
                tws = psT_users2[g4 % 2]
                psT_users2[g4 % 2] = []
                mts = []
                for c in range(4):
                    cc = g4 * 4 + c
                    mts.append(p.emit("tensor", lambda e, cc=cc, c=c, pt=pt, k2s=k2s: e.matmul(
                        pt[:, 128 * c:128 * (c + 1)], k2s[:, 128 * cc:128 * (cc + 1)], id16[:],
                        start=True, stop=True),
                        waits=[l2] + tws + (td_toks if i == 0 and g4 == 0 else []), inc="T"))
                ev = p.emit("scalar", lambda e, g4=g4, pt=pt, k2T=k2T: e.activation(
                    k2T[:, 512 * g4:512 * (g4 + 1)], pt[:], AF.Copy), waits=mts, inc="C")
                psT_users2[g4 % 2].append(ev)
                ktr_evs.append(ev)

            def d_mmgroup(g4):
                for c in range(4 * g4, 4 * g4 + 4):
                    for h in range(2):
                        for j in range(2):
                            mm_last_h[h] = p.emit("tensor", lambda e, c=c, j=j, h=h, k2T=k2T: e.matmul(
                                psD[h][:, 512 * j:512 * (j + 1)],
                                k2T[:, 128 * c:128 * (c + 1)],
                                wT[:, 2048 * c + 1024 * h + 512 * j:2048 * c + 1024 * h + 512 * (j + 1)],
                                start=(c == 0), stop=(c == 31)),
                                waits=(([ktr_evs[g4]] if c == 4 * g4 else []) +
                                       (list(dws) + list(td_toks) if c == 0 else [])), inc="T")

            d_tgroup(0)
            d_tgroup(1)
            for g4 in range(8):
                if g4 + 2 < 8:
                    d_tgroup(g4 + 2)
                d_mmgroup(g4)
            k2r_ring.used(i, ktr_evs[-1])
            x1t = p.emit("vector", lambda e, s=slot: e.tensor_tensor(x1d, s, c_rep[:], OP.add),
                         waits=[ld, y_last], inc="V")
            xbd_ring.used(i, x1t)
            for h in range(2):
                ov = p.emit("vector", lambda e, h=h, i=i: e.scalar_tensor_tensor(
                    yb[:, 1024 * h:1024 * (h + 1)], psD[h][:], dall[:, i:i + 1],
                    x1d[:, 1024 * h:1024 * (h + 1)], OP.mult, OP.add),
                    waits=[mm_last_h[h], x1t, dtok] + yb_users, inc="V")
                psD_users[h].append(ov)
                yb_users = []
                if h == 1:
                    y_last = p.emit("scalar", lambda e, i=i: e.dma_start(
                        y_out[128 * i:128 * (i + 1), :], yb), waits=[ov], inc="SY", amt=16)
                    yb_users = [y_last]

        final = ("SY", p.cnt["SY"])
        p.emit("scalar", lambda e: e.nop(), waits=[final])
        p.emit("sync", lambda e: e.nop(), waits=[final])

        emit = p.emitters(sems)
        block.sync(emit["sync"])
        block.scalar(emit["scalar"])
        block.vector(emit["vector"])
        block.tensor(emit["tensor"])
        block.gpsimd(emit["gpsimd"])

    return nc


_CACHE = {}


def kernel(L0_attn_x, L0_attn_Wq, L0_attn_bq, L0_attn_Wk, L0_attn_bk,
           L0_attn_Wv, L0_attn_bv, L0_attn_Wo, L0_attn_bo,
           L0_ffn_w_up, L0_ffn_w_down):
    x = np.ascontiguousarray(np.asarray(L0_attn_x, dtype=np.float32)).reshape(16384, D)
    Wv_ = np.ascontiguousarray(np.asarray(L0_attn_Wv, dtype=np.float32))
    Wo_ = np.ascontiguousarray(np.asarray(L0_attn_Wo, dtype=np.float32))
    bv = np.asarray(L0_attn_bv, dtype=np.float32)
    bo = np.asarray(L0_attn_bo, dtype=np.float32)
    wuT = np.ascontiguousarray(np.asarray(L0_ffn_w_up, dtype=np.float32).T)
    wdT = np.ascontiguousarray(np.asarray(L0_ffn_w_down, dtype=np.float32).T)
    bvT = np.ascontiguousarray(bv.reshape(16, 128).T)
    boT = np.ascontiguousarray(bo.reshape(16, 128).T)
    id16 = np.eye(128, dtype=np.float16)
    id32 = np.eye(128, dtype=np.float32)
    ones1 = np.ones((1, 128), dtype=np.float32)
    onec = np.ones((128, 1), dtype=np.float32)

    if "nc" not in _CACHE:
        _CACHE["nc"] = build_nc()
    nc = _CACHE["nc"]

    in_maps = []
    for c in range(8):
        b = c // 2
        half = c % 2
        own = x[b * 4096 + half * 2048: b * 4096 + half * 2048 + 2048]
        sib = x[b * 4096 + (1 - half) * 2048: b * 4096 + (1 - half) * 2048 + 2048]
        in_maps.append({
            "x_own": own, "x_sib": sib, "Wv": Wv_, "Wo": Wo_,
            "wuT": wuT, "wdT": wdT, "bvT": bvT, "boT": boT,
            "id16": id16, "id32": id32, "ones1": ones1, "onec": onec,
        })
    res = run_bass_kernel_spmd(nc, in_maps, core_ids=list(range(8)))
    y = np.empty((16384, D), dtype=np.float32)
    for c in range(8):
        b = c // 2
        half = c % 2
        y[b * 4096 + half * 2048: b * 4096 + half * 2048 + 2048] = res.results[c]["y"]
    return y.reshape(4, 4096, D)


# version 24
# speedup vs baseline: 1.0963x; 1.0362x over previous
import numpy as np
import concourse.bass as bass
import concourse.mybir as mybir
from concourse.bass_utils import run_bass_kernel_spmd

F32 = mybir.dt.float32
F16 = mybir.dt.float16
F8 = mybir.dt.float8e4
I32 = mybir.dt.int32
OP = mybir.AluOpType
AF = mybir.ActivationFunctionType
AX = mybir.AxisListType.X

D = 2048
FF = 4096
TOK = 2048
NT = 16
EPS = 1e-5

ENGS = ("sync", "scalar", "vector", "tensor", "gpsimd")
DEBUG = False


class Prog:
    def __init__(self):
        self.ops = {e: [] for e in ENGS}
        self.cnt = {}

    def ch(self, name):
        self.cnt[name] = 0

    ENG_CH = {"vector": "V", "scalar": "C", "gpsimd": "G"}

    def emit(self, eng, fn, waits=(), inc=None, amt=1):
        if inc is None and eng in self.ENG_CH:
            inc = self.ENG_CH[eng]
            amt = 1
        tok = None
        if inc is not None:
            self.cnt[inc] += amt
            tok = (inc, self.cnt[inc])
        ws = tuple(w for w in waits if w is not None)
        self.ops[eng].append((fn, ws, inc, amt))
        return tok

    def barrier(self):
        return [(c, n) for c, n in self.cnt.items()]

    def emitters(self, sems):
        last = {}
        COMPUTE = ("V", "C", "G")

        def mk(name):
            def run(eng):
                pending = None
                for fn, waits, inc, amt in self.ops[name]:
                    if pending is not None:
                        s, v = pending
                        k = (name, s)
                        if last.get(k, 0) < v:
                            eng.wait_ge(sems[s], v)
                            last[k] = v
                        pending = None
                    for (s, v) in waits:
                        k = (name, s)
                        if last.get(k, 0) < v:
                            eng.wait_ge(sems[s], v)
                            last[k] = v
                    inst = fn(eng)
                    if inc is not None:
                        inst.then_inc(sems[inc], amt)
                        if inc in COMPUTE:
                            self_tok = None
                            pending = (inc, self._emitted_cnt[name][inc])
                for _ in ():
                    pass
            return run
        self._emitted_cnt = {e: {} for e in ENGS}
        for e in ENGS:
            cnt = {}
            vals = []
            for fn, waits, inc, amt in self.ops[e]:
                if inc is not None:
                    cnt[inc] = cnt.get(inc, 0) + amt
                    vals.append(cnt[inc])
                else:
                    vals.append(None)
            self._emitted_cnt[e] = (cnt, vals)

        def mk2(name):
            _, vals = self._emitted_cnt[name]

            def run(eng):
                pending = None
                for idx, (fn, waits, inc, amt) in enumerate(self.ops[name]):
                    if pending is not None:
                        s, v = pending
                        k = (name, s)
                        if last.get(k, 0) < v:
                            eng.wait_ge(sems[s], v)
                            last[k] = v
                        pending = None
                    for (s, v) in waits:
                        k = (name, s)
                        if last.get(k, 0) < v:
                            eng.wait_ge(sems[s], v)
                            last[k] = v
                    inst = fn(eng)
                    if inc is not None:
                        inst.then_inc(sems[inc], amt)
                        if inc in COMPUTE:
                            pending = (inc, vals[idx])
            return run
        return {e: mk2(e) for e in ENGS}


class Ring:
    def __init__(self, slots, semnames=None):
        self.slots = list(slots)
        self.n = len(self.slots)
        self.users = [[] for _ in self.slots]
        self.sems = semnames or [None] * self.n

    def acquire(self, i):
        s = i % self.n
        waits = self.users[s]
        self.users[s] = []
        return self.slots[s], waits

    def sem(self, i):
        return self.sems[i % self.n]

    def used(self, i, tok):
        if tok is not None:
            self.users[i % self.n].append(tok)


def build_nc():
    nc = bass.Bass("TRN2", target_bir_lowering=False, detect_race_conditions=False)

    x_own = nc.dram_tensor("x_own", [TOK, D], F32, kind="ExternalInput").ap()
    x_sib = nc.dram_tensor("x_sib", [TOK, D], F32, kind="ExternalInput").ap()
    Wv = nc.dram_tensor("Wv", [D, D], F32, kind="ExternalInput").ap()
    Wo = nc.dram_tensor("Wo", [D, D], F32, kind="ExternalInput").ap()
    wuT = nc.dram_tensor("wuT", [D, FF], F32, kind="ExternalInput").ap()
    wdT = nc.dram_tensor("wdT", [FF, D], F32, kind="ExternalInput").ap()
    bvT_h = nc.dram_tensor("bvT", [128, 16], F32, kind="ExternalInput").ap()
    boT_h = nc.dram_tensor("boT", [128, 16], F32, kind="ExternalInput").ap()
    id16_h = nc.dram_tensor("id16", [128, 128], F16, kind="ExternalInput").ap()
    id32_h = nc.dram_tensor("id32", [128, 128], F32, kind="ExternalInput").ap()
    ones1_h = nc.dram_tensor("ones1", [1, 128], F32, kind="ExternalInput").ap()
    onec_h = nc.dram_tensor("onec", [128, 1], F32, kind="ExternalInput").ap()
    y_out = nc.dram_tensor("y", [TOK, D], F32, kind="ExternalOutput").ap()
    k2_hbm = nc.dram_tensor("k2s", [TOK, FF], F16, kind="Internal").ap()
    td_hbm = nc.dram_tensor("tds", [FF, D], F8, kind="Internal").ap()
    us_hbm = nc.dram_tensor("us", [16, 128], F32, kind="Internal").ap()
    if DEBUG:
        dbg_crep = nc.dram_tensor("dbg_crep", [128, D], F32, kind="ExternalOutput").ap()
        dbg_srep = nc.dram_tensor("dbg_srep", [128, 8], F32, kind="ExternalOutput").ap()
        dbg_kf = nc.dram_tensor("dbg_kf", [128, D], F16, kind="ExternalOutput").ap()
        dbg_hb = nc.dram_tensor("dbg_hb", [128, FF], F32, kind="ExternalOutput").ap()
        dbg_am2 = nc.dram_tensor("dbg_am2", [128, 16], F32, kind="ExternalOutput").ap()
        dbg_wt = nc.dram_tensor("dbg_wt", [128, 4096], F8, kind="ExternalOutput").ap()
        dbg_x1 = nc.dram_tensor("dbg_x1", [128, D], F32, kind="ExternalOutput").ap()
        dbg_tv = nc.dram_tensor("dbg_tv", [128, 64], F32, kind="ExternalOutput").ap()
        dbg_xrow = nc.dram_tensor("dbg_xrow", [1, D], F32, kind="ExternalOutput").ap()
        dbg_rep = nc.dram_tensor("dbg_rep", [128, D], F32, kind="ExternalOutput").ap()
        dbg_ucol = nc.dram_tensor("dbg_ucol", [128, 16], F32, kind="ExternalOutput").ap()
        dbg_uT = nc.dram_tensor("dbg_uT", [16, 128], F32, kind="ExternalOutput").ap()
        dbg_m1 = nc.dram_tensor("dbg_m1", [128, D], F16, kind="ExternalOutput").ap()
    cs_hbm = nc.dram_tensor("cs", [16, 128], F32, kind="Internal").ap()

    p = Prog()
    DMACH = ("LC", "L0", "L1", "W0", "W1", "LB", "LT", "K0", "K1",
             "U0", "U1", "U2", "U3", "L2", "L3", "D0", "D1", "SB", "SY", "SK", "SD0", "SD1", "SG")
    for c in ("T", "V", "C", "G") + DMACH:
        p.ch(c)

    from contextlib import ExitStack
    es = ExitStack()
    with es:
        id16 = es.enter_context(nc.sbuf_tensor("sb_id16", [128, 128], F16))
        id32 = es.enter_context(nc.sbuf_tensor("sb_id32", [128, 128], F32))
        ones1 = es.enter_context(nc.sbuf_tensor("sb_ones1", [1, 128], F32))
        onec = es.enter_context(nc.sbuf_tensor("sb_onec", [128, 1], F32))
        bvT = es.enter_context(nc.sbuf_tensor("sb_bvT", [128, 16], F32))
        boT = es.enter_context(nc.sbuf_tensor("sb_boT", [128, 16], F32))
        c_rep = es.enter_context(nc.sbuf_tensor("c_rep", [128, D], F32))
        am2c = es.enter_context(nc.sbuf_tensor("am2c", [128, 16], F32))
        wabs_u = es.enter_context(nc.sbuf_tensor("wabs_u", [128, 32], F32))
        wabs_d = es.enter_context(nc.sbuf_tensor("wabs_d", [128, 32], F32))
        ucol = es.enter_context(nc.sbuf_tensor("ucol", [128, 16], F32))
        ccol = es.enter_context(nc.sbuf_tensor("ccol", [128, 16], F32))
        uTsb = es.enter_context(nc.sbuf_tensor("uT", [16, 128], F32))
        cTsb = es.enter_context(nc.sbuf_tensor("cT", [16, 128], F32))
        tiny = es.enter_context(nc.sbuf_tensor("tiny", [128, 64], F32))
        tinyC = es.enter_context(nc.sbuf_tensor("tinyC", [128, 16], F32))
        xsall = es.enter_context(nc.sbuf_tensor("xsall", [128, 32], F32))
        ssqall = es.enter_context(nc.sbuf_tensor("ssqall", [128, 32], F32))
        aall = es.enter_context(nc.sbuf_tensor("aall", [128, 32], F32))
        maall = es.enter_context(nc.sbuf_tensor("maall", [128, 32], F32))
        p1s = es.enter_context(nc.sbuf_tensor("p1s", [128, 128], F32))
        srep = es.enter_context(nc.sbuf_tensor("srep", [128, 8], F32))
        dall = es.enter_context(nc.sbuf_tensor("dall", [128, 16], F32))
        t8a = es.enter_context(nc.sbuf_tensor("t8a", [128, 2048], F8))
        mkq = es.enter_context(nc.sbuf_tensor("mkq8", [128, 1024], F8))
        mk8 = es.enter_context(nc.sbuf_tensor("mk88", [128, 1024], F8))
        t8b = es.enter_context(nc.sbuf_tensor("t8b", [128, 2048], F8))
        pf32 = es.enter_context(nc.sbuf_tensor("pf32", [128, 23680], F32))
        pf16 = es.enter_context(nc.sbuf_tensor("pf16", [128, 12288], F16))
        pi32 = es.enter_context(nc.sbuf_tensor("pi32", [128, 2048], I32))
        wT = es.enter_context(nc.sbuf_tensor("wT", [128, 65536], F8))
        PA = es.enter_context(nc.psum_tensor("PA", [128, 2048], F32))
        PM = es.enter_context(nc.psum_tensor("PM", [128, 512], F32))
        PT2 = es.enter_context(nc.psum_tensor("PT2", [128, 512], F32))
        PS = es.enter_context(nc.psum_tensor("PS", [128, 512], F32))
        sems = {}
        for c in ("T", "V", "C", "G") + DMACH:
            sems[c] = es.enter_context(nc.semaphore("sem" + c))
        block = es.enter_context(nc.Block())
        def f32(off, n):
            return pf32[:, off:off + n]

        TC = {}

        def tc(name):
            if name not in TC:
                TC[name] = len(TC)
                assert len(TC) <= 16
            i = TC[name]
            return tinyC[:, i:i + 1]

        TV = {}

        def tv(name):
            if name not in TV:
                TV[name] = len(TV)
                assert len(TV) <= 56
            i = TV[name]
            return tiny[:, i:i + 1]

        onesm = f32(23552, 128)
        om = p.emit("vector", lambda e: e.memset(onesm, 1.0), inc="V")
        for dst, src in ((id16, id16_h), (id32, id32_h), (ones1, ones1_h),
                         (onec, onec_h), (bvT, bvT_h), (boT, boT_h)):
            p.emit("sync", lambda e, d=dst, s=src: e.dma_start(d[:], s[:]), inc="LC", amt=16)
        tok_const = ("LC", p.cnt["LC"])

        xb_ring = Ring([f32(0, D), f32(2048, D), f32(4096, D)], ["L0", "L1", "L2"])
        wu_ring = Ring([f32(6144, 4096), f32(10240, 4096)], ["U0", "U1"])
        wdnB_ring = Ring([f32(14336, 4096), f32(18432, 4096)], ["D0", "D1"])
        cdump = f32(12288, D)
        vdump = f32(14336, D)

        last_mm = None
        last_bs = None

        wdT2 = wdT.rearrange("(a b) c -> a (b c)", b=2)
        td2 = td_hbm.rearrange("(a b) c -> a (b c)", b=2)
        bnb = p1s[:, 64:88]
        wu_i = [0]
        wd_i = [0]
        absu_last = [None]
        absd_last2 = [None]

        def emit_absu(r):
            i = wu_i[0]; wu_i[0] += 1
            slot, ws = wu_ring.acquire(i)
            ld = p.emit("scalar", lambda e, d=slot, r=r: e.dma_start(
                d[:], wuT[128 * r:128 * (r + 1), :]), waits=ws, inc=wu_ring.sem(i), amt=16)
            p.emit("scalar", lambda e, s=slot, r=r: e.activation(
                s[:, 0:2048], s[:, 0:2048], AF.Abs, accum_out=wabs_u[:, 2 * r:2 * r + 1]),
                waits=[ld], inc="C")
            a2 = p.emit("scalar", lambda e, s=slot, r=r: e.activation(
                s[:, 2048:4096], s[:, 2048:4096], AF.Abs, accum_out=wabs_u[:, 2 * r + 1:2 * r + 2]),
                inc="C")
            wu_ring.used(i, a2)
            absu_last[0] = a2

        def emit_absd(r):
            i = wd_i[0]; wd_i[0] += 1
            slot, ws = wdnB_ring.acquire(i)
            ld = p.emit("gpsimd", lambda e, d=slot, r=r: e.dma_start(
                d[:], wdT2[128 * r:128 * (r + 1), :]), waits=ws, inc=wdnB_ring.sem(i), amt=16)
            p.emit("scalar", lambda e, s=slot, r=r: e.activation(
                s[:, 0:2048], s[:, 0:2048], AF.Abs, accum_out=wabs_d[:, 2 * r:2 * r + 1]),
                waits=[ld], inc="C")
            a2 = p.emit("scalar", lambda e, s=slot, r=r: e.activation(
                s[:, 2048:4096], s[:, 2048:4096], AF.Abs, accum_out=wabs_d[:, 2 * r + 1:2 * r + 2]),
                inc="C")
            wdnB_ring.used(i, a2)
            absd_last2[0] = a2

        ma_last = None
        for t in range(32):
            src_t = x_own if t < 16 else x_sib
            r0 = (t % 16) * 128
            slot, ws = xb_ring.acquire(t)
            ld = p.emit("sync", lambda e, d=slot, s=src_t, r=r0: e.dma_start(d[:], s[r:r + 128, :]),
                        waits=ws + [tok_const], inc=xb_ring.sem(t), amt=16)
            for q in range(4):
                p.emit("vector", lambda e, xbv=slot, q=q: e.bn_stats(
                    bnb[:, 6 * q:6 * (q + 1)], xbv[:, 512 * q:512 * (q + 1)]),
                    waits=[ld], inc="V")
            mvt = p1s[:, 2 * t:2 * t + 2]
            ag = p.emit("vector", lambda e, mvt=mvt: e.bn_aggr(mvt, bnb), inc="V")
            p.emit("vector", lambda e, mvt=mvt: e.tensor_scalar(
                tv("ve"), mvt[:, 1:2], 1.0, EPS, OP.mult, OP.add))
            rv = p.emit("vector", lambda e: e.reciprocal(tv("rq"), tv("ve")), inc="V")
            rc = p.emit("scalar", lambda e, t=t: e.sqrt(aall[:, t:t + 1], tv("rq")),
                        waits=[rv], inc="C")
            ma_last = p.emit("vector", lambda e, mvt=mvt, t=t: e.tensor_tensor(
                maall[:, t:t + 1], mvt[:, 0:1], aall[:, t:t + 1], OP.mult),
                waits=[rc], inc="V")
            for q in range(4):
                last_mm = p.emit("tensor", lambda e, xbv=slot, q=q, t=t,
                                 s0=(t == 0), s1=(t == 31): e.matmul(
                    PA[0:1, 512 * q:512 * (q + 1)], aall[:, t:t + 1],
                    xbv[:, 512 * q:512 * (q + 1)], start=s0, stop=s1),
                    waits=[rc, ld], inc="T")
                xb_ring.used(t, last_mm)
            xb_ring.used(t, ag)
            if t % 2 == 0:
                emit_absu(t // 2)
            else:
                emit_absd(t // 2)

        svd = p.emit("vector", lambda e: e.tensor_reduce(tv("wsd"), wabs_d[:], AX, OP.add),
                     waits=[absd_last2[0]], inc="V")
        tkd = p.emit("tensor", lambda e: e.matmul(PS[:, 17:18], onesm, tv("wsd"),
                                                  start=True, stop=True), waits=[svd, om], inc="T")
        rrd = p.emit("vector", lambda e: e.tensor_copy(tv("draw"), PS[:, 17:18]),
                     waits=[tkd], inc="V")
        p.emit("vector", lambda e: e.tensor_scalar(srep[:, 2:3], tv("draw"),
                                                   1.0 / (D * FF), EPS, OP.mult, OP.max))
        sdn_tok = p.emit("vector", lambda e: e.reciprocal(srep[:, 3:4], srep[:, 2:3]), inc="V")
        thd_tok = p.emit("vector", lambda e: e.tensor_scalar(
            srep[:, 5:6], srep[:, 2:3], 0.5, None, OP.mult), inc="V")
        sdn_rep2 = srep[:, 2:3]
        ivd_rep2 = srep[:, 3:4]
        thd_rep = srep[:, 5:6]

        sv = p.emit("vector", lambda e: e.tensor_reduce(tv("wsu"), wabs_u[:], AX, OP.add),
                    waits=[absu_last[0]], inc="V")
        tksu = p.emit("tensor", lambda e: e.matmul(PS[:, 32:33], onesm, tv("wsu"),
                                                   start=True, stop=True),
                      waits=[sv, om], inc="T")
        rr = p.emit("vector", lambda e: e.tensor_copy(tv("wraw"), PS[:, 32:33]),
                    waits=[tksu], inc="V")
        p.emit("vector", lambda e: e.tensor_scalar(srep[:, 0:1], tv("wraw"),
                                                   1.0 / (D * FF), EPS, OP.mult, OP.max))
        bu = p.emit("vector", lambda e: e.reciprocal(srep[:, 1:2], srep[:, 0:1]), inc="V")
        thu_tok = p.emit("vector", lambda e: e.tensor_scalar(
            srep[:, 4:5], srep[:, 0:1], 0.5, None, OP.mult), inc="V")
        sup_rep = srep[:, 0:1]
        ivu_rep = srep[:, 1:2]
        thu_rep = srep[:, 4:5]

        last_bs = p.emit("tensor", lambda e: e.matmul(PS[:, 0:32], onesm, maall[:],
                                                      start=True, stop=True),
                         waits=[rr, om], inc="T")

        m1b = pf16[:, 0:4096]
        m2b = pf16[:, 4096:8192]
        t8_rP = Ring([t8a, t8b], ["SD0", "SD1"])
        ternV_last = None
        ternd_P_toks = []
        for r in range(16):
            i = wu_i[0]; wu_i[0] += 1
            slot, ws = wu_ring.acquire(i)
            ld = p.emit("scalar", lambda e, d=slot, r=r: e.dma_start(
                d[:], wuT[128 * r:128 * (r + 1), :]), waits=ws + [thu_tok], inc=wu_ring.sem(i), amt=16)
            p.emit("vector", lambda e, s=slot: e.tensor_scalar(
                m1b, s, -1.0, thu_rep, OP.mult, OP.is_gt), waits=[ld, thu_tok])
            ternV_last = p.emit("vector", lambda e, s=slot, r=r: e.scalar_tensor_tensor(
                wT[:, 4096 * r:4096 * (r + 1)], s, thu_rep, m1b, OP.is_gt, OP.subtract), inc="V")
            wu_ring.used(i, ternV_last)

            i = wd_i[0]; wd_i[0] += 1
            slot, ws = wdnB_ring.acquire(i)
            ld = p.emit("gpsimd", lambda e, d=slot, r=r: e.dma_start(
                d[:], wdT2[128 * r:128 * (r + 1), :]), waits=ws + [thd_tok],
                inc=wdnB_ring.sem(i), amt=16)
            p.emit("vector", lambda e, s=slot: e.tensor_scalar(
                m1b, s, -1.0, thd_rep, OP.mult, OP.is_gt), waits=[ld, thd_tok])
            ta, taw = t8_rP.acquire(2 * r)
            t1 = p.emit("vector", lambda e, s=slot, o=ta: e.scalar_tensor_tensor(
                o[:], s[:, 0:2048], thd_rep, m1b[:, 0:2048], OP.is_gt, OP.subtract),
                waits=taw, inc="V")
            sa = p.emit("sync", lambda e, r=r, o=ta: e.dma_start(
                td2[128 * r:128 * (r + 1), 0:2048], o[:]), waits=[t1],
                inc=t8_rP.sem(2 * r), amt=16)
            t8_rP.used(2 * r, sa)
            tb, tbw = t8_rP.acquire(2 * r + 1)
            t2 = p.emit("vector", lambda e, s=slot, o=tb: e.scalar_tensor_tensor(
                o[:], s[:, 2048:4096], thd_rep, m1b[:, 2048:4096], OP.is_gt, OP.subtract),
                waits=tbw, inc="V")
            sb = p.emit("sync", lambda e, r=r, o=tb: e.dma_start(
                td2[128 * r:128 * (r + 1), 2048:4096], o[:]), waits=[t2],
                inc=t8_rP.sem(2 * r + 1), amt=16)
            t8_rP.used(2 * r + 1, sb)
            wdnB_ring.used(i, t2)
            ternV_last = t2
            ternd_P_toks.append(sb)

        xrow = pf32[0:1, 18432:18432 + D]
        e1 = p.emit("vector", lambda e: e.tensor_copy(xrow, PA[0:1, :]), waits=[last_mm], inc="V")
        bs = p.emit("vector", lambda e: e.tensor_reduce(tv("bsr"), PS[:, 0:32], AX, OP.add),
                    waits=[last_bs], inc="V")
        rep = f32(16384, D)
        brd = []
        for q in range(4):
            brd.append(p.emit("tensor", lambda e, q=q: e.matmul(
                PA[:, 512 * q:512 * (q + 1)], ones1[:], xrow[0:1, 512 * q:512 * (q + 1)],
                start=True, stop=True), waits=[e1], inc="T"))
        re1 = p.emit("vector", lambda e: e.tensor_scalar(rep, PA[:], tv("bsr"), 1.0 / 4096.0,
                                                         OP.subtract, OP.mult),
                     waits=brd + [bs], inc="V")
        if DEBUG:
            p.emit("scalar", lambda e: e.dma_start(dbg_xrow[:], rep[0:1, :]), waits=[re1], inc="SG", amt=16)

        if DEBUG:
            p.emit("scalar", lambda e: e.dma_start(dbg_rep[:], rep), waits=[re1], inc="SG", amt=16)
        wvo_ring = Ring([f32(8192, D), f32(10240, D)], ["W0", "W1"])
        ttr_last = None
        for t in range(16):
            slot, ws = wvo_ring.acquire(t)
            ld = p.emit("sync", lambda e, d=slot, t=t: e.dma_start(d[:], Wv[128 * t:128 * (t + 1), :]),
                        waits=ws, inc=wvo_ring.sem(t), amt=16)
            ttr_last = p.emit("vector", lambda e, s=slot, t=t: e.scalar_tensor_tensor(
                vdump, s, 1.0, rep, OP.mult, OP.mult, accum_out=ucol[:, t:t + 1]),
                waits=[ld, re1], inc="V")
            wvo_ring.used(t, ttr_last)
        ub = p.emit("vector", lambda e: e.tensor_tensor(ucol[:], ucol[:], bvT[:], OP.add),
                    waits=[ttr_last], inc="V")
        if DEBUG:
            p.emit("scalar", lambda e: e.dma_start(dbg_ucol[:], ucol[:]), waits=[ub], inc="SG", amt=16)
        tk = p.emit("tensor", lambda e: e.matmul(PM[0:16, 0:128], ucol[:], id32[:],
                                                 start=True, stop=True), waits=[ub], inc="T")
        ue = p.emit("vector", lambda e: e.tensor_copy(uTsb[:], PM[0:16, 0:128]), waits=[tk], inc="V")
        if DEBUG:
            p.emit("scalar", lambda e: e.dma_start(dbg_uT[:], uTsb[:]), waits=[ue], inc="SG", amt=16)
        ust = p.emit("scalar", lambda e: e.dma_start(us_hbm[:], uTsb[:]), waits=[ue], inc="SB", amt=16)
        uld = p.emit("sync", lambda e: e.dma_start(
            xrow, us_hbm.rearrange("a b -> (a b)")[None, :]), waits=[ust, e1] + brd, inc="LB", amt=16)
        brd2 = []
        for q in range(4):
            brd2.append(p.emit("tensor", lambda e, q=q: e.matmul(
                PA[:, 512 * q:512 * (q + 1)], ones1[:], xrow[0:1, 512 * q:512 * (q + 1)],
                start=True, stop=True), waits=[uld, re1], inc="T"))
        re2 = p.emit("scalar", lambda e: e.activation(rep, PA[:], AF.Copy),
                     waits=brd2 + [ttr_last], inc="C")

        tto_last = None
        for t in range(16):
            slot, ws = wvo_ring.acquire(16 + t)
            ld = p.emit("sync", lambda e, d=slot, t=t: e.dma_start(d[:], Wo[128 * t:128 * (t + 1), :]),
                        waits=ws, inc=wvo_ring.sem(16 + t), amt=16)
            tto_last = p.emit("vector", lambda e, s=slot, t=t: e.scalar_tensor_tensor(
                vdump, s, 1.0, rep, OP.mult, OP.mult, accum_out=ccol[:, t:t + 1]),
                waits=[ld, re2], inc="V")
            wvo_ring.used(16 + t, tto_last)
        cb = p.emit("vector", lambda e: e.tensor_tensor(ccol[:], ccol[:], boT[:], OP.add),
                    waits=[tto_last], inc="V")
        tk = p.emit("tensor", lambda e: e.matmul(PM[0:16, 0:128], ccol[:], id32[:],
                                                 start=True, stop=True), waits=[cb, ue], inc="T")
        ce = p.emit("vector", lambda e: e.tensor_copy(cTsb[:], PM[0:16, 0:128]), waits=[tk], inc="V")
        cst = p.emit("scalar", lambda e: e.dma_start(cs_hbm[:], cTsb[:]), waits=[ce], inc="SB", amt=16)
        cld = p.emit("sync", lambda e: e.dma_start(
            xrow, cs_hbm.rearrange("a b -> (a b)")[None, :]), waits=[cst, uld] + brd2, inc="LB", amt=16)
        brd3 = []
        for q in range(4):
            brd3.append(p.emit("tensor", lambda e, q=q: e.matmul(
                PA[:, 512 * q:512 * (q + 1)], ones1[:], xrow[0:1, 512 * q:512 * (q + 1)],
                start=True, stop=True), waits=[cld, re2], inc="T"))
        crep_tok = p.emit("scalar", lambda e: e.activation(c_rep[:], PA[:], AF.Copy),
                          waits=brd3, inc="C")

        bar = p.barrier()

        xbu_ring = Ring([f32(0, D), f32(2048, D)], ["L0", "L1"])
        x1b = f32(4096, D)
        hb = f32(8192, FF)
        zq = f32(18432, 1024)
        mq = f32(19456, 1024)
        e8 = f32(20480, 1024)
        z8 = f32(21504, 1024)
        s8 = f32(22528, 1024)
        wdn_ring = Ring([f32(14336, D), f32(16384, D)], ["D0", "D1"])
        kf = pf16[:, 0:2048]
        dm1 = pf16[:, 2048:4096]
        dm2 = pf16[:, 4096:6144]
        kT_one = pf16[:, 6144:8192]
        k2b = pf16[:, 8192:12288]
        ziq = pi32[:, 0:1024]
        zi8 = pi32[:, 1024:2048]

        psU = [PA[:, 0:1024], PA[:, 1024:2048]]
        psU_users = [[crep_tok], [crep_tok]]
        psT = [PM, PT2]
        psT_users = [[ce, bu], [ce, bu]]

        wdn_items = []
        wdn_iter = iter(wdn_items)
        wdn_i = [0]
        absd_last = [None]
        sdn_done = [None]
        ternd_toks = []
        t8_ring = Ring([t8a, t8b], ["SD0", "SD1"])
        sdn_rep = srep[:, 2:3]
        ivd_rep = srep[:, 3:4]
        dm_prev = [None]

        def emit_wdn(n):
            for _ in range(n):
                it = next(wdn_iter, None)
                if it is None:
                    return
                kind, r = it
                if kind == "sdn":
                    sv = p.emit("vector", lambda e: e.tensor_reduce(tv("wsd"), wabs_d[:], AX, OP.add),
                                waits=[absd_last2[0]] + bar, inc="V")
                    tk = p.emit("tensor", lambda e: e.matmul(PS[:, 17:18], onesm, tv("wsd"),
                                                             start=True, stop=True),
                                waits=[sv] + bar, inc="T")
                    rr = p.emit("vector", lambda e: e.tensor_copy(tv("draw"), PS[:, 17:18]),
                                waits=[tk], inc="V")
                    p.emit("vector", lambda e: e.tensor_scalar(srep[:, 2:3], tv("draw"),
                                                               1.0 / (D * FF), EPS, OP.mult, OP.max))
                    sdn_done[0] = p.emit("vector", lambda e: e.reciprocal(srep[:, 3:4], srep[:, 2:3]),
                                         inc="V")
                else:
                    i = wdn_i[0]; wdn_i[0] += 1
                    slot, ws = wdn_ring.acquire(i)
                    ld = p.emit("sync", lambda e, d=slot, r=r: e.dma_start(
                        d[:], wdT[128 * r:128 * (r + 1), :]), waits=ws + [sdn_done[0]],
                        inc=wdn_ring.sem(i), amt=16)
                    g1 = p.emit("vector", lambda e, s=slot: e.tensor_scalar(
                        dm1, s, ivd_rep, 0.5, OP.mult, OP.is_gt),
                        waits=[ld, sdn_done[0], dm_prev[0]])
                    g2 = p.emit("vector", lambda e, s=slot: e.tensor_scalar(
                        dm2, s, ivd_rep, -0.5, OP.mult, OP.is_lt), waits=[ld])
                    t8, t8w = t8_ring.acquire(len(ternd_toks))
                    tt = p.emit("vector", lambda e, o=t8: e.tensor_tensor(o[:], dm1, dm2, OP.subtract),
                                waits=[g2, g1] + t8w, inc="V")
                    dm_prev[0] = tt
                    st = p.emit("scalar", lambda e, r=r, o=t8: e.dma_start(
                        td_hbm[128 * r:128 * (r + 1), :], o[:]), waits=[tt],
                        inc=t8_ring.sem(len(ternd_toks)), amt=16)
                    t8_ring.used(len(ternd_toks), st)
                    wdn_ring.used(i, tt)
                    ternd_toks.append(st)

        prev_sq = [None]
        kf_free = [None]
        k2b_free = [None]
        st_state = {}

        front_nb = [None]

        def emit_front(i):
            slot, ws = xbu_ring.acquire(i)
            ld = p.emit("sync", lambda e, d=slot, i=i: e.dma_start(d[:], x_own[128 * i:128 * (i + 1), :]),
                        waits=ws + bar, inc=xbu_ring.sem(i), amt=16)
            x1 = p.emit("vector", lambda e, s=slot: e.scalar_tensor_tensor(
                x1b, s, 0.0, c_rep[:], OP.add, OP.add, accum_out=tv("xs")),
                waits=[ld, crep_tok, prev_sq[0]] + bar, inc="V")
            xbu_ring.used(i, x1)
            p.emit("vector", lambda e: e.tensor_reduce(tv("rmax"), x1b, AX, OP.max))
            p.emit("vector", lambda e: e.tensor_reduce(tv("rmin"), x1b, AX, OP.min))
            prev_sq[0] = p.emit("scalar", lambda e: e.activation(
                cdump, x1b, AF.Square, accum_out=tc("ssq")), waits=[x1] + bar, inc="C")
            p.emit("vector", lambda e: e.tensor_scalar(tv("mean"), tv("xs"), 1.0 / D, None, OP.mult))
            p.emit("vector", lambda e: e.tensor_tensor(tv("me2"), tv("mean"), tv("mean"), OP.mult))
            p.emit("vector", lambda e: e.tensor_scalar(tv("v1"), tc("ssq"), 1.0 / D, None, OP.mult),
                   waits=[prev_sq[0]])
            p.emit("vector", lambda e: e.tensor_tensor(tv("var"), tv("v1"), tv("me2"), OP.subtract))
            p.emit("vector", lambda e: e.tensor_scalar(tv("ve"), tv("var"), EPS, None, OP.add))
            rv = p.emit("vector", lambda e: e.reciprocal(tv("rq"), tv("ve")), inc="V")
            rs = p.emit("scalar", lambda e: e.sqrt(tc("rstd"), tv("rq")), waits=[rv], inc="C")
            p.emit("vector", lambda e: e.tensor_tensor(tv("t1"), tv("rmax"), tv("mean"), OP.subtract),
                   waits=[rs])
            p.emit("vector", lambda e: e.tensor_tensor(tv("t2"), tv("mean"), tv("rmin"), OP.subtract))
            p.emit("vector", lambda e: e.tensor_tensor(tv("t3"), tv("t1"), tv("t2"), OP.max))
            p.emit("vector", lambda e: e.tensor_tensor(tv("amax"), tv("t3"), tc("rstd"), OP.mult))
            amc = tv("amc%d" % (i % 2))
            front_amc = amc
            p.emit("vector", lambda e, amc=amc: e.tensor_scalar(amc, tv("amax"), EPS, None, OP.max))
            p.emit("vector", lambda e, amc=amc: e.tensor_scalar(tv("r127"), amc, 1.0 / 127.0, None, OP.mult))
            p.emit("vector", lambda e: e.reciprocal(tv("scl"), tv("r127")))
            p.emit("vector", lambda e: e.tensor_tensor(tv("sa"), tc("rstd"), tv("scl"), OP.mult))
            p.emit("vector", lambda e: e.tensor_tensor(tv("nb0"), tv("mean"), tv("sa"), OP.mult))
            nb = p.emit("vector", lambda e: e.tensor_scalar(tv("nbias"), tv("nb0"), -1.0, None, OP.mult),
                        inc="V")
            front_nb[0] = nb

        def emit_front_b(i):
            nb = front_nb[0]
            kq_last = None
            zcs = {}

            def q_zv(q):
                s_ = q % 2
                bsl = slice(512 * s_, 512 * (s_ + 1))
                zh = zq[:, bsl]
                sl = slice(512 * q, 512 * (q + 1))
                zv = p.emit("vector", lambda e, sl=sl, zh=zh: e.tensor_scalar(
                    zh, x1b[:, sl], tv("sa"), tv("nbias"), OP.mult, OP.add),
                    waits=[nb] + ([kf_free[0]] if q <= 1 else []), inc="V")
                zcs[q] = p.emit("scalar", lambda e, zh=zh, zih=ziq[:, bsl]: e.activation(
                    zih, zh, AF.Copy), waits=[zv] + bar, inc="C")

            def q_fix(q):
                s_ = q % 2
                bsl = slice(512 * s_, 512 * (s_ + 1))
                zh = zq[:, bsl]; mh = mq[:, bsl]; zih = ziq[:, bsl]
                sl = slice(512 * q, 512 * (q + 1))
                p.emit("vector", lambda e, zh=zh, zih=zih, mh2=mh: e.tensor_tensor(
                    mh2, zh, zih, OP.subtract), waits=[zcs[q]])
                p.emit("vector", lambda e, mh2=mh, zih=zih, zh=zh: e.scalar_tensor_tensor(
                    zh, mh2, -0.5, zih, OP.is_lt, OP.subtract))
                return p.emit("vector", lambda e, sl=sl, mh2=mh, zh=zh: e.scalar_tensor_tensor(
                    kf[:, sl], mh2, 0.5, zh, OP.is_gt, OP.subtract), inc="V")

            fix_toks = {}
            q_zv(0); q_zv(1)
            fix_toks[0] = q_fix(0)
            q_zv(2)
            fix_toks[1] = q_fix(1)
            q_zv(3)
            fix_toks[2] = q_fix(2)
            fix_toks[3] = q_fix(3)
            kq_last = fix_toks[3]
            kT, kws = kT_one, []
            ktr_evs = []
            tr_last = None
            for g4 in range(4):
                pt = psT[g4 % 2]
                tws = psT_users[g4 % 2]
                psT_users[g4 % 2] = []
                mts = []
                for c in range(4):
                    cc = g4 * 4 + c
                    tr_last = p.emit("tensor", lambda e, cc=cc, c=c, pt=pt: e.matmul(
                        pt[:, 128 * c:128 * (c + 1)], kf[:, 128 * cc:128 * (cc + 1)], id16[:],
                        start=True, stop=True),
                        waits=[fix_toks[g4]] + tws + kws + bar, inc="T")
                    mts.append(tr_last)
                ev = p.emit("scalar", lambda e, g4=g4, pt=pt, kT=kT: e.activation(
                    kT[:, 512 * g4:512 * (g4 + 1)], pt[:], AF.Copy), waits=mts, inc="C")
                psT_users[g4 % 2].append(ev)
                ktr_evs.append(ev)
            kf_free[0] = tr_last
            st_state[i] = {"kT": kT, "ktr": ktr_evs, "amc": tv("amc%d" % (i % 2))}

        def emit_gamma(i):
            st = st_state[i]
            sg_prev = st_state.get(i - 1, {}).get("sg")
            p.emit("vector", lambda e: e.tensor_scalar(tv("Mr"), tv("Mq0"), 0.0, None, OP.max),
                   waits=st["mr"] + ([sg_prev] if sg_prev else []))
            p.emit("vector", lambda e, amc=st["amc"]: e.tensor_scalar(
                tv("ba"), amc, sup_rep, 1.0 / 127.0, OP.mult, OP.mult))
            p.emit("vector", lambda e: e.tensor_tensor(tv("bm"), tv("ba"), tv("Mr"), OP.mult))
            p.emit("vector", lambda e: e.tensor_tensor(tv("am2"), tv("bm"), tv("bm"), OP.mult))
            p.emit("vector", lambda e, i=i: e.tensor_scalar(
                am2c[:, i:i + 1], tv("am2"), EPS, None, OP.max))
            p.emit("vector", lambda e: e.tensor_tensor(tv("rb2"), tv("ba"), tv("ba"), OP.mult))
            p.emit("vector", lambda e, i=i: e.reciprocal(tv("rc"), am2c[:, i:i + 1]))
            p.emit("vector", lambda e: e.tensor_tensor(tv("g0"), tv("rb2"), tv("rc"), OP.mult))
            gv = p.emit("vector", lambda e: e.tensor_scalar(tv("g127"), tv("g0"), 127.0, None, OP.mult),
                        inc="V")
            st["sg"] = p.emit("scalar", lambda e: e.sqrt(tc("sqg"), tv("g127")), waits=[gv], inc="C")
            st["pk20"] = st_state.get(i - 1, {}).get("pk20")
            st["pk21"] = st_state.get(i - 1, {}).get("pk21")

        def emit_epi_chunk(i, o):
            st = st_state[i]
            s_ = o % 2
            bsl = slice(512 * s_, 512 * (s_ + 1))
            eh = e8[:, bsl]; zh = z8[:, bsl]; zih = zi8[:, bsl]
            sh = s8[:, bsl]; mh = mk8[:, 512 * s_:512 * (s_ + 1)]
            sl = slice(512 * o, 512 * (o + 1))
            pk = st["pk2%d" % s_]
            ee = p.emit("scalar", lambda e, sl=sl, eh=eh: e.activation(
                eh, hb[:, sl], AF.Relu, scale=tc("sqg")), waits=[st["sg"]], inc="C")
            zz = p.emit("scalar", lambda e, eh=eh, zh=zh: e.square(zh, eh), waits=[ee, pk], inc="C")
            zc = p.emit("scalar", lambda e, zh=zh, zih=zih: e.activation(zih, zh, AF.Copy),
                        waits=[zz], inc="C")
            p.emit("vector", lambda e, zh=zh, zih=zih, sh=sh: e.tensor_tensor(
                sh, zh, zih, OP.subtract),
                waits=[zc] + ([k2b_free[0]] if o <= 1 else []))
            p.emit("vector", lambda e, sh=sh, zih=zih, zh=zh: e.scalar_tensor_tensor(
                zh, sh, -0.5, zih, OP.is_lt, OP.subtract))
            tok = p.emit("vector", lambda e, sl=sl, sh=sh, zh=zh: e.scalar_tensor_tensor(
                k2b[:, sl], sh, 0.5, zh, OP.is_gt, OP.subtract), waits=bar, inc="V")
            st["pk2%d" % s_] = tok
            st.setdefault("k2toks", {})[o // 2 if False else o] = tok
            return tok

        def emit_mm_quarter(i, q):
            st = st_state[i]
            pu = psU[q % 2]
            uws = psU_users[q % 2]
            psU_users[q % 2] = []
            mm_last = None
            for c in range(16):
                for j in range(2):
                    w_ = []
                    if q == 0 and c % 4 == 0 and j == 0:
                        w_ = [st["ktr"][c // 4]]
                    if c == 0 and j == 0:
                        w_ = w_ + uws + [ternV_last] + bar
                    mm_last = p.emit("tensor", lambda e, c=c, j=j, q=q, pu=pu, kT=st["kT"]: e.matmul(
                        pu[:, 512 * j:512 * (j + 1)],
                        kT[:, 128 * c:128 * (c + 1)],
                        wT[:, 4096 * c + 1024 * q + 512 * j:4096 * c + 1024 * q + 512 * (j + 1)],
                        start=(c == 0), stop=(c == 15)),
                        waits=w_, inc="T")
            hb_guard = (st_state.get(i - 1, {}).get("mr") or [None] * 4)[q]
            ev = p.emit("scalar", lambda e, q=q, pu=pu: e.activation(
                hb[:, 1024 * q:1024 * (q + 1)], pu[:], AF.Copy),
                waits=[mm_last, hb_guard], inc="C")
            psU_users[q % 2].append(ev)
            st.setdefault("evs", []).append(ev)

        for i in range(NT + 1):
            if i < NT:
                emit_front(i)
                emit_front_b(i)
            if i >= 1:
                emit_gamma(i - 1)
            for q in range(4):
                if i >= 1:
                    emit_epi_chunk(i - 1, 2 * q)
                    emit_epi_chunk(i - 1, 2 * q + 1)
                if i < NT:
                    emit_mm_quarter(i, q)
            if i < NT:
                st = st_state[i]
                mr = p.emit("vector", lambda e: e.tensor_reduce(
                    tv("Mq0"), hb[:], AX, OP.max), waits=st["evs"], inc="V")
                st["mr"] = [mr, mr, mr, mr]
            if i >= 1:
                k2b_free[0] = p.emit("scalar", lambda e, j=i - 1: e.dma_start(
                    k2_hbm[128 * j:128 * (j + 1), :], k2b),
                    waits=[st_state[i - 1]["k2toks"][6], st_state[i - 1]["k2toks"][7]],
                    inc="SK", amt=16)
                st_state.pop(i - 2, None)
            emit_wdn(4)
        emit_wdn(99)

        bar2 = p.barrier()

        td_toks = list(bar2)
        for f in range(32):
            td_toks.append(p.emit("sync", lambda e, f=f: e.dma_start(
                wT[:, 2048 * f:2048 * (f + 1)], td_hbm[128 * f:128 * (f + 1), :]),
                waits=bar2, inc="LT", amt=16))
        dtok = p.emit("vector", lambda e: e.tensor_scalar(dall[:], am2c[:], sdn_rep2, 1.0 / 127.0,
                                                          OP.mult, OP.mult), waits=bar2, inc="V")

        xbd_ring = Ring([f32(0, D), f32(2048, D)], ["L0", "L1"])
        x1d = f32(4096, D)
        yb = f32(6144, D)
        k2r_ring = Ring([pf16[:, 0:4096], pf16[:, 4096:8192]], ["K0", "K1"])
        k2T_one = pf16[:, 8192:12288]
        psD = [PA[:, 0:1024], PA[:, 1024:2048]]
        psD_users = [list(bar2), list(bar2)]
        psT_users2 = [list(bar2), list(bar2)]
        yb_users = []
        y_last = None
        for i in range(NT):
            slot, ws = xbd_ring.acquire(i)
            ld = p.emit("sync", lambda e, d=slot, i=i: e.dma_start(d[:], x_own[128 * i:128 * (i + 1), :]),
                        waits=ws + bar2, inc=xbd_ring.sem(i), amt=16)
            k2s, k2w = k2r_ring.acquire(i)
            l2 = p.emit("sync", lambda e, d=k2s, i=i: e.dma_start(d[:], k2_hbm[128 * i:128 * (i + 1), :]),
                        waits=k2w + bar2, inc=k2r_ring.sem(i), amt=16)
            k2T = k2T_one
            ktr_evs = []
            dws = psD_users[0] + psD_users[1]
            psD_users[0] = []
            psD_users[1] = []
            mm_last_h = [None, None]

            def d_tgroup(g4):
                pt = psT[g4 % 2]
                tws = psT_users2[g4 % 2]
                psT_users2[g4 % 2] = []
                mts = []
                for c in range(4):
                    cc = g4 * 4 + c
                    mts.append(p.emit("tensor", lambda e, cc=cc, c=c, pt=pt, k2s=k2s: e.matmul(
                        pt[:, 128 * c:128 * (c + 1)], k2s[:, 128 * cc:128 * (cc + 1)], id16[:],
                        start=True, stop=True),
                        waits=[l2] + tws + (td_toks if i == 0 and g4 == 0 else []), inc="T"))
                ev = p.emit("scalar", lambda e, g4=g4, pt=pt, k2T=k2T: e.activation(
                    k2T[:, 512 * g4:512 * (g4 + 1)], pt[:], AF.Copy), waits=mts, inc="C")
                psT_users2[g4 % 2].append(ev)
                ktr_evs.append(ev)

            def d_mmgroup(g4):
                for c in range(4 * g4, 4 * g4 + 4):
                    for h in range(2):
                        for j in range(2):
                            mm_last_h[h] = p.emit("tensor", lambda e, c=c, j=j, h=h, k2T=k2T: e.matmul(
                                psD[h][:, 512 * j:512 * (j + 1)],
                                k2T[:, 128 * c:128 * (c + 1)],
                                wT[:, 2048 * c + 1024 * h + 512 * j:2048 * c + 1024 * h + 512 * (j + 1)],
                                start=(c == 0), stop=(c == 31)),
                                waits=(([ktr_evs[g4]] if c == 4 * g4 else []) +
                                       (list(dws) + list(td_toks) if c == 0 else [])), inc="T")

            d_tgroup(0)
            d_tgroup(1)
            for g4 in range(8):
                if g4 + 2 < 8:
                    d_tgroup(g4 + 2)
                d_mmgroup(g4)
            k2r_ring.used(i, ktr_evs[-1])
            x1t = p.emit("vector", lambda e, s=slot: e.tensor_tensor(x1d, s, c_rep[:], OP.add),
                         waits=[ld, y_last], inc="V")
            xbd_ring.used(i, x1t)
            for h in range(2):
                ov = p.emit("vector", lambda e, h=h, i=i: e.scalar_tensor_tensor(
                    yb[:, 1024 * h:1024 * (h + 1)], psD[h][:], dall[:, i:i + 1],
                    x1d[:, 1024 * h:1024 * (h + 1)], OP.mult, OP.add),
                    waits=[mm_last_h[h], x1t, dtok] + yb_users, inc="V")
                psD_users[h].append(ov)
                yb_users = []
                if h == 1:
                    y_last = p.emit("scalar", lambda e, i=i: e.dma_start(
                        y_out[128 * i:128 * (i + 1), :], yb), waits=[ov], inc="SY", amt=16)
                    yb_users = [y_last]

        final = ("SY", p.cnt["SY"])
        p.emit("scalar", lambda e: e.nop(), waits=[final])
        p.emit("sync", lambda e: e.nop(), waits=[final])

        emit = p.emitters(sems)
        block.sync(emit["sync"])
        block.scalar(emit["scalar"])
        block.vector(emit["vector"])
        block.tensor(emit["tensor"])
        block.gpsimd(emit["gpsimd"])

    return nc


_CACHE = {}


def kernel(L0_attn_x, L0_attn_Wq, L0_attn_bq, L0_attn_Wk, L0_attn_bk,
           L0_attn_Wv, L0_attn_bv, L0_attn_Wo, L0_attn_bo,
           L0_ffn_w_up, L0_ffn_w_down):
    x = np.ascontiguousarray(np.asarray(L0_attn_x, dtype=np.float32)).reshape(16384, D)
    Wv_ = np.ascontiguousarray(np.asarray(L0_attn_Wv, dtype=np.float32))
    Wo_ = np.ascontiguousarray(np.asarray(L0_attn_Wo, dtype=np.float32))
    bv = np.asarray(L0_attn_bv, dtype=np.float32)
    bo = np.asarray(L0_attn_bo, dtype=np.float32)
    wuT = np.ascontiguousarray(np.asarray(L0_ffn_w_up, dtype=np.float32).T)
    wdT = np.ascontiguousarray(np.asarray(L0_ffn_w_down, dtype=np.float32).T)
    bvT = np.ascontiguousarray(bv.reshape(16, 128).T)
    boT = np.ascontiguousarray(bo.reshape(16, 128).T)
    id16 = np.eye(128, dtype=np.float16)
    id32 = np.eye(128, dtype=np.float32)
    ones1 = np.ones((1, 128), dtype=np.float32)
    onec = np.ones((128, 1), dtype=np.float32)

    if "nc" not in _CACHE:
        _CACHE["nc"] = build_nc()
    nc = _CACHE["nc"]

    in_maps = []
    for c in range(8):
        b = c // 2
        half = c % 2
        own = x[b * 4096 + half * 2048: b * 4096 + half * 2048 + 2048]
        sib = x[b * 4096 + (1 - half) * 2048: b * 4096 + (1 - half) * 2048 + 2048]
        in_maps.append({
            "x_own": own, "x_sib": sib, "Wv": Wv_, "Wo": Wo_,
            "wuT": wuT, "wdT": wdT, "bvT": bvT, "boT": boT,
            "id16": id16, "id32": id32, "ones1": ones1, "onec": onec,
        })
    res = run_bass_kernel_spmd(nc, in_maps, core_ids=list(range(8)))
    y = np.empty((16384, D), dtype=np.float32)
    for c in range(8):
        b = c // 2
        half = c % 2
        y[b * 4096 + half * 2048: b * 4096 + half * 2048 + 2048] = res.results[c]["y"]
    return y.reshape(4, 4096, D)
